# version 03
# baseline (speedup 1.0000x reference)
import contextlib
import numpy as np
import ml_dtypes
import concourse.bass as bass
import concourse.mybir as mybir
from concourse.bass_utils import run_bass_kernel_spmd

F32 = mybir.dt.float32
BF16 = mybir.dt.bfloat16
U32 = mybir.dt.uint32
U8 = mybir.dt.uint8
AF = mybir.ActivationFunctionType
OP = mybir.AluOpType
AX = mybir.AxisListType

BATCH, T, D, DI, S, E, H, NCORE = 8, 4096, 1024, 2048, 16, 8, 2048, 8
B = 128
NB = T // B
TC = 8
QC = 32
HRING = 2
TG = 512
NTG = T // TG
NTC = TG // 128
EXP_M20 = float(np.exp(np.float32(-20.0)))
DO_PH1 = True
DO_PH2 = True
DO_PH3 = True


class Sem:
    def __init__(self, es, nc, name):
        self.h = es.enter_context(nc.semaphore(name))
        self.n = 0


_NOTRACK = {"wait_ge", "nop", "engine_nop"}


class EngLast:

    def __init__(self, eng):
        object.__setattr__(self, "_e", eng)
        object.__setattr__(self, "last", None)

    def __getattr__(self, name):
        f = getattr(self._e, name)
        if callable(f) and name not in _NOTRACK and not name.startswith("_"):
            def wrap(*a, **k):
                r = f(*a, **k)
                if r is not None and hasattr(r, "then_inc"):
                    object.__setattr__(self, "last", r)
                return r
            return wrap
        return f


class WBlk:
    def __init__(self, blk):
        self._b = blk

    def _mk(self, name):
        dec = getattr(self._b, name)

        def deco(fn):
            def inner(eng):
                return fn(EngLast(eng))
            return dec(inner)
        return deco

    @property
    def vector(self):
        return self._mk("vector")

    @property
    def scalar(self):
        return self._mk("scalar")

    @property
    def tensor(self):
        return self._mk("tensor")

    @property
    def gpsimd(self):
        return self._mk("gpsimd")

    @property
    def sync(self):
        return self._mk("sync")


def build(debug=False):
    nc = bass.Bass("TRN2", target_bir_lowering=False)

    x_in = nc.dram_tensor("x", [T, D], F32, kind="ExternalInput").ap()
    ipwT = nc.dram_tensor("ipwT", [D, 2 * DI], F32, kind="ExternalInput").ap()
    opwT = nc.dram_tensor("opwT", [DI, D], F32, kind="ExternalInput").ap()
    xpw = nc.dram_tensor("xpw", [DI, 80], F32, kind="ExternalInput").ap()
    dtw = nc.dram_tensor("dtw", [1, DI], F32, kind="ExternalInput").ap()
    gwT = nc.dram_tensor("gwT", [D, E], F32, kind="ExternalInput").ap()
    w1 = nc.dram_tensor("w1", [E, D, H], F32, kind="ExternalInput").ap()
    w2 = nc.dram_tensor("w2", [E, H, D], F32, kind="ExternalInput").ap()
    g_convb = nc.dram_tensor("g_convb", [128, 16], F32, kind="ExternalInput").ap()
    g_dtb = nc.dram_tensor("g_dtb", [128, 16], F32, kind="ExternalInput").ap()
    g_D = nc.dram_tensor("g_D", [128, 16], F32, kind="ExternalInput").ap()
    g_lng = nc.dram_tensor("g_lng", [128, 16], F32, kind="ExternalInput").ap()
    g_lnb = nc.dram_tensor("g_lnb", [128, 16], F32, kind="ExternalInput").ap()
    g_convw = nc.dram_tensor("g_convw", [128, 16, 4], F32, kind="ExternalInput").ap()
    alog = nc.dram_tensor("alog", [1, S], F32, kind="ExternalInput").ap()
    gamma_b = nc.dram_tensor("gamma_b", [128, D], F32, kind="ExternalInput").ap()
    beta_b = nc.dram_tensor("beta_b", [128, D], F32, kind="ExternalInput").ap()
    ones_col = nc.dram_tensor("ones_col", [128, 1], F32, kind="ExternalInput").ap()
    ones_row = nc.dram_tensor("ones_row", [1, 128], F32, kind="ExternalInput").ap()
    sel16 = nc.dram_tensor("sel16", [S, S * 128], F32, kind="ExternalInput").ap()
    cconst = nc.dram_tensor("cconst", [128, 8], F32, kind="ExternalInput").ap()
    ident_in = nc.dram_tensor("ident_in", [128, 128], F32, kind="ExternalInput").ap()

    out = nc.dram_tensor("out", [T, D], F32, kind="ExternalOutput").ap()
    okind = "ExternalOutput" if debug else "Internal"
    xn_dram = nc.dram_tensor("xn_dram", [T, D], BF16, kind="Internal").ap()
    g_dram = nc.dram_tensor("g_dram", [T, D], F32, kind=okind).ap()
    s_dram = nc.dram_tensor("s_dram", [T, D], F32, kind=okind).ap()
    xn_f32 = nc.dram_tensor("xn_f32", [T, D], F32, kind=okind).ap() if debug else None
    opw_bf = nc.dram_tensor("opw_bf", [DI, D], BF16, kind="Internal").ap()

    es = contextlib.ExitStack()
    with es:
        ipw_sb = es.enter_context(nc.sbuf_tensor("ipw_sb", [128, 8, 2 * DI], BF16))
        convb_g = es.enter_context(nc.sbuf_tensor("convb_g", [128, 16], F32))
        dtb_g = es.enter_context(nc.sbuf_tensor("dtb_g", [128, 16], F32))
        D_g = es.enter_context(nc.sbuf_tensor("D_g", [128, 16], F32))
        lng_g = es.enter_context(nc.sbuf_tensor("lng_g", [128, 16], F32))
        lnb_g = es.enter_context(nc.sbuf_tensor("lnb_g", [128, 16], F32))
        convw_g = es.enter_context(nc.sbuf_tensor("convw_g", [128, 16, 4], F32))
        alog_sb = es.enter_context(nc.sbuf_tensor("alog_sb", [1, S], F32))
        arow_sb = es.enter_context(nc.sbuf_tensor("arow_sb", [1, S], F32))
        abc_sb = es.enter_context(nc.sbuf_tensor("abc_sb", [128, S], F32))
        gam_sb = es.enter_context(nc.sbuf_tensor("gam_sb", [128, D], F32))
        bet_sb = es.enter_context(nc.sbuf_tensor("bet_sb", [128, D], F32))
        onesc_sb = es.enter_context(nc.sbuf_tensor("onesc_sb", [128, 1], F32))
        onesr_sb = es.enter_context(nc.sbuf_tensor("onesr_sb", [1, 128], F32))
        sel_sb = es.enter_context(nc.sbuf_tensor("sel_sb", [S, S * 128], F32))
        cc_sb = es.enter_context(nc.sbuf_tensor("cc_sb", [128, 8], F32))
        xpw_sb = es.enter_context(nc.sbuf_tensor("xpw_sb", [128, 16, 80], F32))
        dtw_sb = es.enter_context(nc.sbuf_tensor("dtw_sb", [1, DI], F32))
        gw_sb = es.enter_context(nc.sbuf_tensor("gw_sb", [128, 8, E], F32))
        ident_sb = es.enter_context(nc.sbuf_tensor("ident_sb", [128, 128], F32))
        mtv_sb = es.enter_context(nc.sbuf_tensor("mtv_sb", [128, NB, E], F32))
        dtraw_sb = es.enter_context(nc.sbuf_tensor("dtraw_sb", [1, B], F32))
        strow_sb = es.enter_context(nc.sbuf_tensor("strow_sb", [1, 4 * B], F32))
        BC_sb = es.enter_context(nc.sbuf_tensor("BC_sb", [16, 2 * B], F32))
        small_sb = es.enter_context(nc.sbuf_tensor("small_sb", [128, 32], F32))
        smallu_sb = es.enter_context(nc.sbuf_tensor("smallu_sb", [128, 8], U32))

        ARENA_BYTES = 108 * 1024
        arena = es.enter_context(nc.sbuf_tensor("arena", [128, ARENA_BYTES], U8))
        aap = arena.ap()

        class Carver:
            def __init__(self):
                self.off = 0

            def take(self, shape, dtype, name=""):
                n = int(np.prod(shape[1:]))
                nbytes = n * mybir.dt.size(dtype)
                off = (self.off + 31) // 32 * 32
                assert off + nbytes <= ARENA_BYTES, f"arena overflow at {name}: {off}+{nbytes}"
                self.off = off + nbytes
                ap = aap[:, off:off + nbytes].bitcast(dtype)
                if len(shape) > 2:
                    pat = "p (" + " ".join(f"d{i}" for i in range(1, len(shape))) + ") -> p " + \
                          " ".join(f"d{i}" for i in range(1, len(shape)))
                    kw = {f"d{i}": shape[i] for i in range(1, len(shape) - 1)}
                    ap = ap.rearrange(pat, **kw)
                return ap

        c = Carver()
        _xt = c.take([128, D], F32, "xt0")
        xtile = [_xt, _xt]
        psq = c.take([128, D], F32, "psq")
        xnf32 = c.take([128, D], F32, "xnf32")
        xnT = psq.rearrange("p (a b) -> p a b", a=8)
        xn_out = c.take([128, D], BF16, "xn_out")
        gate_e = c.take([128, E], F32, "gate_e")
        _xnt = c.take([128, 8, TG], BF16, "xnt0")
        xnt = [_xnt, _xnt]
        w1_sb = c.take([128, 8, H], BF16, "w1_sb")
        w2_sb = c.take([128, 16, D], BF16, "w2_sb")
        heg = c.take([128, 16, TG], BF16, "heg")
        gtile = c.take([128, D], F32, "gtile")
        gate_sb = c.take([128, E], F32, "gate_sb")
        ph01_high = c.off

        c = Carver()
        xnp = c.take([128, 8, B], BF16, "xnp")
        x_in0 = c.take([128, 16, B + 3], F32, "x_in0")
        xc_r = c.take([128, 16, B], F32, "xc_r")
        u_r = c.take([128, 16, B], F32, "u_r")
        zs_r = c.take([128, 16, B], F32, "zs_r")
        dt_r = c.take([128, 16, B], F32, "dt_r")
        dtu_r = c.take([128, 16, B], F32, "dtu_r")
        dA_c = c.take([128, S, 16, TC], F32, "dA_c")
        dBu_c = c.take([128, S, 16, TC], F32, "dBu_c")
        hring = c.take([128, HRING, 256], F32, "hring")
        v1_s = c.take([128, S, 16], F32, "v1")
        v2_s = c.take([128, 256], F32, "v2")
        ypre = c.take([128, 16, S], F32, "ypre")
        ybuf = c.take([128, B, 16], F32, "ybuf")
        ysq = c.take([128, B, 16], F32, "ysq")
        yds = c.take([128, 2 * B], F32, "yds")
        yln = ysq
        Bb_sb = c.take([128, S, QC], F32, "Bb_sb")
        y2bf = c.take([128, 16, B], BF16, "y2bf")
        opw_c = [c.take([128, 512], BF16, f"opw{i}") for i in range(6)]
        s_tile = c.take([128, D], F32, "s_tile")
        spt = [c.take([128, B], F32, f"spt{i}") for i in range(3)]
        ph2_high = c.off

        c = Carver()
        p3x = c.take([128, D], F32, "p3x")
        p3g = c.take([128, D], F32, "p3g")
        p3s = c.take([128, D], F32, "p3s")
        p3t = c.take([128, D], F32, "p3t")
        p3o = c.take([128, D], F32, "p3o")
        ph3_high = c.off

        if debug:
            print(f"[build] arena usage: ph01={ph01_high} ph2={ph2_high} ph3={ph3_high} of {ARENA_BYTES}")
            print(f"[build] sbuf remaining: {nc.sbuf_bytes_remaining}")

        pb = [es.enter_context(nc.psum_tensor(f"pb{i}", [128, 512], F32)) for i in range(8)]

        LD = Sem(es, nc, "LD")
        LDg = Sem(es, nc, "LDg")
        STm = Sem(es, nc, "STm")
        STg = Sem(es, nc, "STg")
        TE = Sem(es, nc, "TE")
        AC = Sem(es, nc, "AC")
        VE = Sem(es, nc, "VE")
        PO = Sem(es, nc, "PO")
        allsems = [LD, LDg, STm, STg, TE, AC, VE, PO]

        blk = WBlk(es.enter_context(nc.Block()))

        def wait_all(eng):
            for sm in allsems:
                if sm.n > 0:
                    eng.wait_ge(sm.h, sm.n)

        bank_rel = [None] * 8

        def te_take(t, i):
            if bank_rel[i] is not None:
                sm, n = bank_rel[i]
                t.wait_ge(sm.h, n)
                bank_rel[i] = None

        @blk.sync
        def _(sync):
            for dst, src in [(convb_g, g_convb), (dtb_g, g_dtb), (D_g, g_D), (lng_g, g_lng),
                             (lnb_g, g_lnb), (alog_sb, alog), (gam_sb, gamma_b), (bet_sb, beta_b),
                             (onesc_sb, ones_col), (onesr_sb, ones_row), (sel_sb, sel16),
                             (cc_sb, cconst), (dtw_sb, dtw)]:
                sync.dma_start(out=dst[:], in_=src).then_inc(LD.h, 16)
                LD.n += 16
            sync.dma_start(out=convw_g[:], in_=g_convw).then_inc(LD.h, 16)
            LD.n += 16
            sync.dma_start(out=xpw_sb[:], in_=xpw.rearrange("(c p) m -> p c m", p=128)).then_inc(LD.h, 16)
            LD.n += 16

        @blk.gpsimd
        def _(g):
            g.dma_start(out=gw_sb[:], in_=gwT.rearrange("(c p) m -> p c m", p=128)).then_inc(LDg.h, 16)
            LDg.n += 16
            g.dma_start(out=ident_sb[:], in_=ident_in).then_inc(LDg.h, 16)
            LDg.n += 16
            g.dma_start(out=opw_bf, in_=opwT).then_inc(LDg.h, 16)
            LDg.n += 16

        nc.const_aps.register(F32, 1.0, cc_sb.ap()[:, 0:1])
        nc.const_aps.register(F32, 1e-5, cc_sb.ap()[:, 1:2])
        nc.const_aps.register(F32, -0.5, cc_sb.ap()[:, 2:3])
        nc.const_aps.register(F32, -1.0, cc_sb.ap()[:, 3:4])

        @blk.vector
        def _(v):
            wait_all(v)
            v.tensor_scalar(arow_sb[:], alog_sb[:], 5.0, None, OP.min)
            v.last.then_inc(VE.h, 1)
        VE.n += 1

        @blk.scalar
        def _(s):
            s.wait_ge(VE.h, VE.n)
            s.activation(out=alog_sb[:], in_=arow_sb[:], func=AF.Exp)
            s.last.then_inc(AC.h, 1)
        AC.n += 1

        @blk.tensor
        def _(t):
            t.wait_ge(AC.h, AC.n)
            t.matmul(pb[5][:, 0:S], onesr_sb[:], alog_sb[:], start=True, stop=True)
            t.last.then_inc(TE.h, 1)
        TE.n += 1

        @blk.vector
        def _(v):
            v.wait_ge(TE.h, TE.n)
            v.tensor_scalar(abc_sb[:], pb[5][:, 0:S], -1.0, None, OP.mult)
            v.last.then_inc(VE.h, 1)
        VE.n += 1
        bank_rel[5] = (VE, VE.n)

        for tb in range(NB):
            buf = xtile[tb % 2]

            @blk.sync
            def _(sync, tb=tb, buf=buf):
                wait_all(sync)
                sync.dma_start(out=buf, in_=x_in[tb * B:(tb + 1) * B, :]).then_inc(LD.h, 16)
            LD.n += 16

            @blk.scalar
            def _(s, buf=buf):
                s.wait_ge(LD.h, LD.n)
                s.activation(out=psq, in_=buf, func=AF.Square)
                s.last.then_inc(AC.h, 1)
            AC.n += 1

            @blk.vector
            def _(v, buf=buf):
                v.wait_ge(AC.h, AC.n)
                mu = small_sb.ap()[:, 0:1]
                m2 = small_sb.ap()[:, 1:2]
                nvar = small_sb.ap()[:, 2:3]
                v.reduce_sum(mu, buf[:, None, :], axis=AX.X).then_inc(VE.h, 1)
                v.reduce_sum(m2, psq[:, None, :], axis=AX.X).then_inc(VE.h, 1)
                VE.n += 2
                v.wait_ge(VE.h, VE.n)
                v.tensor_scalar(mu, mu, 1.0 / D, None, OP.mult).then_inc(VE.h, 1)
                v.tensor_scalar(m2, m2, 1.0 / D, None, OP.mult).then_inc(VE.h, 1)
                VE.n += 2
                v.wait_ge(VE.h, VE.n)
                v.scalar_tensor_tensor(nvar, mu, mu, m2, OP.mult, OP.subtract).then_inc(VE.h, 1)
                VE.n += 1

            @blk.scalar
            def _(s):
                s.wait_ge(VE.h, VE.n)
                nvar = small_sb.ap()[:, 2:3]
                lnv = small_sb.ap()[:, 17:18]
                rstd = small_sb.ap()[:, 3:4]
                s.activation(out=lnv, in_=nvar, func=AF.Ln, scale=-1.0, bias=1e-5).then_inc(AC.h, 1)
                AC.n += 1
                s.wait_ge(AC.h, AC.n)
                s.activation(out=rstd, in_=lnv, func=AF.Exp, scale=-0.5).then_inc(AC.h, 1)
                AC.n += 1

            @blk.vector
            def _(v, buf=buf):
                v.wait_ge(AC.h, AC.n)
                v.wait_ge(STm.h, STm.n)
                v.tensor_scalar(xnf32, buf, small_sb.ap()[:, 0:1], small_sb.ap()[:, 3:4],
                                OP.subtract, OP.mult).then_inc(VE.h, 1)
                VE.n += 1
                v.wait_ge(VE.h, VE.n)
                v.tensor_copy(xn_out, xnf32).then_inc(VE.h, 1)
                VE.n += 1

            @blk.sync
            def _(sync, tb=tb):
                sync.wait_ge(VE.h, VE.n)
                sync.dma_start(out=xn_dram[tb * B:(tb + 1) * B, :], in_=xn_out).then_inc(STm.h, 16)
            STm.n += 16

            for dc in range(8):
                bk = dc % 2

                @blk.tensor
                def _(t, dc=dc, bk=bk):
                    if dc == 0:
                        t.wait_ge(VE.h, VE.n)
                        t.wait_ge(LDg.h, LDg.n)
                    te_take(t, bk)
                    t.transpose(pb[bk][:, 0:128], xnf32[:, dc * 128:(dc + 1) * 128], ident_sb[:])
                    t.last.then_inc(TE.h, 1)
                TE.n += 1

                @blk.scalar
                def _(s, dc=dc, bk=bk):
                    s.wait_ge(TE.h, TE.n)
                    s.activation(out=xnT[:, dc, :], in_=pb[bk][:, 0:128], func=AF.Copy)
                    s.last.then_inc(AC.h, 1)
                AC.n += 1
                bank_rel[bk] = (AC, AC.n)

            @blk.tensor
            def _(t):
                t.wait_ge(AC.h, AC.n)
                te_take(t, 5)
                for kc in range(8):
                    t.matmul(pb[5][:, 0:E], xnT[:, kc, :], gw_sb.ap()[:, kc, :],
                             start=(kc == 0), stop=(kc == 7))
                t.last.then_inc(TE.h, 1)
            TE.n += 1

            @blk.vector
            def _(v):
                v.wait_ge(TE.h, TE.n)
                mx0 = small_sb.ap()[:, 4:5]
                v.tensor_copy(gate_sb, pb[5][:, 0:E]).then_inc(VE.h, 1)
                VE.n += 1
                v.wait_ge(VE.h, VE.n)
                v.reduce_max(mx0, gate_sb[:, None, :], axis=AX.X).then_inc(VE.h, 1)
                VE.n += 1
                v.wait_ge(VE.h, VE.n)
                v.tensor_scalar(small_sb.ap()[:, 12:13], mx0, -1.0, None, OP.mult).then_inc(VE.h, 1)
                VE.n += 1
            bank_rel[5] = (VE, VE.n)

            @blk.scalar
            def _(s):
                s.wait_ge(VE.h, VE.n)
                s.activation(out=gate_e, in_=gate_sb, func=AF.Exp, bias=small_sb.ap()[:, 12:13])
                s.last.then_inc(AC.h, 1)
            AC.n += 1

            @blk.vector
            def _(v, tb=tb):
                v.wait_ge(AC.h, AC.n)
                ssum = small_sb.ap()[:, 13:14]
                tv = small_sb.ap()[:, 14:15]
                mx0 = small_sb.ap()[:, 4:5]
                v.reduce_sum(ssum, gate_e[:, None, :], axis=AX.X).then_inc(VE.h, 1)
                VE.n += 1
                v.wait_ge(VE.h, VE.n)
                v.tensor_scalar(tv, ssum, 1e-8, 1.0, OP.mult, OP.add).then_inc(VE.h, 1)
                VE.n += 1
                v.wait_ge(VE.h, VE.n)
                v.reciprocal(tv, tv).then_inc(VE.h, 1)
                VE.n += 1
                v.wait_ge(VE.h, VE.n)
                for e in range(E):
                    v.scalar_tensor_tensor(mtv_sb.ap()[:, tb, e:e + 1], gate_sb[:, e:e + 1],
                                           mx0, tv, OP.is_equal, OP.mult).then_inc(VE.h, 1)
                    VE.n += 1

        if debug:
            @blk.gpsimd
            def _(g):
                wait_all(g)
                g.dma_start(out=xn_f32, in_=xn_dram).then_inc(STg.h, 16)
            STg.n += 16

        _ph1 = DO_PH1
        def load_xnt(tg, buf):
            @blk.sync
            def _(sync, tg=tg, buf=buf):
                wait_all(sync)
                for dc in range(8):
                    sync.dma_start(
                        out=buf[:, dc, :],
                        in_=xn_dram[tg * TG:(tg + 1) * TG, dc * 128:(dc + 1) * 128],
                        transpose=True,
                    ).then_inc(LD.h, 16)
            LD.n += 8 * 16

        for e in range(E if DO_PH1 else 0):
            @blk.gpsimd
            def _(g, e=e):
                wait_all(g)
                g.dma_start(out=w1_sb, in_=w1[e].rearrange("(c p) h -> p c h", p=128)).then_inc(LDg.h, 16)
                g.dma_start(out=w2_sb, in_=w2[e].rearrange("(c p) h -> p c h", p=128)).then_inc(LDg.h, 16)
            LDg.n += 32

            for tg in range(NTG):
                buf = xnt[tg % 2]
                load_xnt(tg, buf)

                for hc in range(16):
                    bk = hc % 2

                    @blk.tensor
                    def _(t, buf=buf, hc=hc, bk=bk):
                        if hc == 0:
                            t.wait_ge(LD.h, LD.n)
                            t.wait_ge(LDg.h, LDg.n)
                        te_take(t, bk)
                        for kc in range(8):
                            t.matmul(pb[bk][:, 0:TG], w1_sb[:, kc, hc * 128:(hc + 1) * 128],
                                     buf[:, kc, 0:TG], start=(kc == 0), stop=(kc == 7))
                        t.last.then_inc(TE.h, 1)
                    TE.n += 1

                    @blk.scalar
                    def _(s, hc=hc, bk=bk):
                        s.wait_ge(TE.h, TE.n)
                        s.activation(out=heg[:, hc, :], in_=pb[bk][:, 0:TG], func=AF.Gelu)
                        s.last.then_inc(AC.h, 1)
                    AC.n += 1
                    bank_rel[bk] = (AC, AC.n)

                for tc in range(NTC):
                    tbi = tg * NTC + tc

                    @blk.tensor
                    def _(t, tc=tc):
                        t.wait_ge(AC.h, AC.n)
                        te_take(t, 2)
                        te_take(t, 3)
                        for half in range(2):
                            for kc in range(16):
                                t.matmul(pb[2 + half][:, 0:512],
                                         heg[:, kc, tc * 128:(tc + 1) * 128],
                                         w2_sb[:, kc, half * 512:(half + 1) * 512],
                                         start=(kc == 0), stop=(kc == 15))
                        t.last.then_inc(TE.h, 1)
                    TE.n += 1

                    @blk.scalar
                    def _(s, e=e, tbi=tbi):
                        s.wait_ge(TE.h, TE.n)
                        s.wait_ge(STg.h, STg.n)
                        mtv = mtv_sb.ap()[:, tbi, e:e + 1]
                        s.activation(out=gtile[:, 0:512], in_=pb[2][:, 0:512], func=AF.Copy,
                                     scale=mtv).then_inc(AC.h, 1)
                        s.activation(out=gtile[:, 512:1024], in_=pb[3][:, 0:512], func=AF.Copy,
                                     scale=mtv).then_inc(AC.h, 1)
                        AC.n += 2
                    bank_rel[2] = (AC, AC.n)
                    bank_rel[3] = (AC, AC.n)

                    @blk.gpsimd
                    def _(g, e=e, tbi=tbi):
                        g.wait_ge(AC.h, AC.n)
                        if e == 0:
                            g.dma_start(out=g_dram[tbi * B:(tbi + 1) * B, :], in_=gtile).then_inc(STg.h, 16)
                        else:
                            g.dma_start(out=g_dram[tbi * B:(tbi + 1) * B, :], in_=gtile,
                                        accum_op=OP.add).then_inc(STg.h, 16)
                    STg.n += 16

        @blk.gpsimd
        def _(g):
            wait_all(g)
            g.dma_start(out=ipw_sb[:], in_=ipwT.rearrange("(c p) m -> p c m", p=128)).then_inc(LDg.h, 16)
        LDg.n += 16

        @blk.vector
        def _(v):
            wait_all(v)
            v.memset(hring[:, HRING - 1, :], 0.0)
            v.memset(x_in0[:, :, 0:3], 0.0)
            v.last.then_inc(VE.h, 1)
        VE.n += 1

        for tb in range(NB if DO_PH2 else 0):
            t0g = tb * B

            @blk.sync
            def _(sync, t0g=t0g):
                wait_all(sync)
                for dc in range(8):
                    sync.dma_start(
                        out=xnp[:, dc, :],
                        in_=xn_dram[t0g:t0g + B, dc * 128:(dc + 1) * 128],
                        transpose=True,
                    ).then_inc(LD.h, 16)
            LD.n += 8 * 16

            for mc in range(32):
                bk = mc % 4

                @blk.tensor
                def _(t, mc=mc, bk=bk):
                    if mc == 0:
                        wait_all(t)
                    te_take(t, bk)
                    for kc in range(8):
                        t.matmul(pb[bk][:, 0:B], ipw_sb[:, kc, mc * 128:(mc + 1) * 128],
                                 xnp[:, kc, :], start=(kc == 0), stop=(kc == 7))
                    t.last.then_inc(TE.h, 1)
                TE.n += 1

                @blk.scalar
                def _(s, mc=mc, bk=bk):
                    s.wait_ge(TE.h, TE.n)
                    if mc < 16:
                        s.activation(out=x_in0[:, mc, 3:3 + B], in_=pb[bk][:, 0:B], func=AF.Copy)
                    else:
                        s.activation(out=zs_r[:, mc - 16, :], in_=pb[bk][:, 0:B], func=AF.Silu)
                    s.last.then_inc(AC.h, 1)
                AC.n += 1
                bank_rel[bk] = (AC, AC.n)

            @blk.vector
            def _(v):
                v.wait_ge(AC.h, AC.n)
                for dc in range(16):
                    wcol = convw_g.ap()[:, dc, :]
                    v.tensor_scalar(xc_r[:, dc, :], x_in0[:, dc, 0:B], wcol[:, 0:1],
                                    convb_g.ap()[:, dc:dc + 1], OP.mult, OP.add)
                for k in range(1, 4):
                    for dc in range(16):
                        wcol = convw_g.ap()[:, dc, :]
                        r = v.scalar_tensor_tensor(xc_r[:, dc, :], x_in0[:, dc, k:k + B],
                                                   wcol[:, k:k + 1], xc_r[:, dc, :], OP.mult, OP.add)
                        if k == 3:
                            r.then_inc(VE.h, 1)
                            VE.n += 1
                v.tensor_copy(x_in0[:, :, 0:3], x_in0[:, :, B:B + 3]).then_inc(VE.h, 1)
                VE.n += 1

            @blk.scalar
            def _(s):
                s.wait_ge(VE.h, VE.n)
                s.activation(out=u_r[:], in_=xc_r[:], func=AF.Silu)
                s.last.then_inc(AC.h, 1)
            AC.n += 1

            @blk.tensor
            def _(t):
                t.wait_ge(AC.h, AC.n)
                te_take(t, 4)
                for kc in range(16):
                    t.matmul(pb[4][0:80, 0:B], xpw_sb[:, kc, :], u_r[:, kc, :],
                             start=(kc == 0), stop=(kc == 15))
                t.last.then_inc(TE.h, 1)
            TE.n += 1

            @blk.vector
            def _(v):
                v.wait_ge(TE.h, TE.n)
                v.tensor_copy(dtraw_sb[:], pb[4][0:1, 0:B]).then_inc(VE.h, 1)
                v.tensor_copy(BC_sb.ap()[:, 0:B], pb[4][32:48, 0:B]).then_inc(VE.h, 1)
                v.tensor_copy(BC_sb.ap()[:, B:2 * B], pb[4][64:80, 0:B]).then_inc(VE.h, 1)
                VE.n += 3
            bank_rel[4] = (VE, VE.n)

            for dc in range(16):
                @blk.tensor
                def _(t, dc=dc):
                    if dc == 0:
                        t.wait_ge(VE.h, VE.n)
                    te_take(t, 5)
                    t.matmul(pb[5][:, 0:B], dtw_sb.ap()[:, dc * 128:(dc + 1) * 128],
                             dtraw_sb[:], start=True, stop=True)
                    t.last.then_inc(TE.h, 1)
                TE.n += 1

                @blk.scalar
                def _(s, dc=dc):
                    s.wait_ge(TE.h, TE.n)
                    s.wait_ge(VE.h, VE.n)
                    dtb_col = dtb_g.ap()[:, dc:dc + 1]
                    s.activation(out=spt[0], in_=pb[5][:, 0:B], func=AF.Abs, bias=dtb_col).then_inc(AC.h, 1)
                    AC.n += 1
                    s.wait_ge(AC.h, AC.n)
                    s.activation(out=spt[1], in_=spt[0], func=AF.Exp, scale=-1.0).then_inc(AC.h, 1)
                    AC.n += 1
                    s.wait_ge(AC.h, AC.n)
                    s.activation(out=spt[2], in_=spt[1], func=AF.Ln, bias=1.0).then_inc(AC.h, 1)
                    s.activation(out=spt[0], in_=pb[5][:, 0:B], func=AF.Relu, bias=dtb_col).then_inc(AC.h, 1)
                    AC.n += 2
                bank_rel[5] = (AC, AC.n)

                @blk.vector
                def _(v, dc=dc):
                    v.wait_ge(AC.h, AC.n)
                    v.tensor_tensor(dt_r[:, dc, :], spt[2], spt[0], OP.add).then_inc(VE.h, 1)
                    VE.n += 1
                    v.wait_ge(VE.h, VE.n)
                    v.tensor_scalar(dt_r[:, dc, :], dt_r[:, dc, :], 10.0, 1e-4,
                                    OP.min, OP.max).then_inc(VE.h, 1)
                    VE.n += 1

            @blk.gpsimd
            def _(g):
                g.wait_ge(VE.h, VE.n)
                g.wait_ge(AC.h, AC.n)
                g.tensor_tensor(dtu_r[:], dt_r[:], u_r[:], OP.mult)
                g.last.then_inc(PO.h, 1)
            PO.n += 1

            for q in range(4):
                @blk.tensor
                def _(t, q=q):
                    if q == 0:
                        t.wait_ge(PO.h, PO.n)
                    te_take(t, 6)
                    te_take(t, 7)
                    for si in range(S):
                        t.matmul(pb[6][:, si * QC:(si + 1) * QC],
                                 sel_sb.ap()[:, si * 128:(si + 1) * 128],
                                 BC_sb.ap()[:, q * QC:(q + 1) * QC], start=True, stop=True)
                        t.matmul(pb[7][:, si * QC:(si + 1) * QC],
                                 sel_sb.ap()[:, si * 128:(si + 1) * 128],
                                 BC_sb.ap()[:, B + q * QC:B + (q + 1) * QC], start=True, stop=True)
                    t.last.then_inc(TE.h, 1)
                TE.n += 1

                @blk.scalar
                def _(s):
                    s.wait_ge(TE.h, TE.n)
                    s.wait_ge(PO.h, PO.n)
                    s.activation(out=Bb_sb.rearrange("p a b -> p (a b)"),
                                 in_=pb[6].ap()[:, 0:S * QC], func=AF.Copy)
                    s.last.then_inc(AC.h, 1)
                AC.n += 1
                bank_rel[6] = (AC, AC.n)

                for ch in range(QC // TC):
                    tq0 = q * QC + ch * TC

                    @blk.scalar
                    def _(s, tq0=tq0):
                        s.wait_ge(VE.h, VE.n)
                        s.wait_ge(AC.h, AC.n)
                        for si in range(S):
                            s.activation(out=dA_c[:, si, :, :],
                                         in_=dt_r[:, :, tq0:tq0 + TC],
                                         func=AF.Exp, scale=abc_sb.ap()[:, si:si + 1]).then_inc(AC.h, 1)
                            AC.n += 1

                    @blk.gpsimd
                    def _(g, tq0=tq0, ch=ch):
                        g.wait_ge(AC.h, AC.n)
                        tqc = ch * TC
                        bb = Bb_sb[:, :, None, tqc:tqc + TC].broadcast_to([128, S, 16, TC])
                        du = dtu_r[:, None, :, tq0:tq0 + TC].broadcast_to([128, S, 16, TC])
                        g.tensor_tensor(dBu_c[:], du, bb, OP.mult).then_inc(PO.h, 1)
                        PO.n += 1
                        g.wait_ge(PO.h, PO.n)
                        g.tensor_scalar(dBu_c.rearrange("p a b t -> p (a b t)"),
                                        dBu_c.rearrange("p a b t -> p (a b t)"),
                                        -10.0, None, OP.max).then_inc(PO.h, 1)
                        PO.n += 1

                    @blk.vector
                    def _(v, tb=tb, tq0=tq0):
                        v.wait_ge(PO.h, PO.n)
                        for trel in range(TC):
                            tglob = tb * B + tq0 + trel
                            hprev = hring[:, (tglob - 1) % HRING, :].rearrange("p (s c) -> p s c", s=S)
                            hcur_f = hring[:, tglob % HRING, :]
                            v.scalar_tensor_tensor(v1_s[:], dA_c[:, :, :, trel], EXP_M20,
                                                   hprev, OP.max, OP.mult)
                            v.scalar_tensor_tensor(v2_s.rearrange("p (s c) -> p s c", s=S),
                                                   dBu_c[:, :, :, trel], 10.0,
                                                   v1_s[:], OP.min, OP.add)
                            v.tensor_scalar(hcur_f, v2_s, 100.0, -100.0, OP.min, OP.max)
                            tloc = tq0 + trel
                            h_cs = hring[:, tglob % HRING, :].rearrange("p (s c) -> p c s", s=S)
                            crow = pb[7].ap()[:, 0:S * QC].rearrange("p (s tq) -> p s tq", tq=QC)[:, :, tloc % QC]
                            cbt = crow[:, None, :].broadcast_to([128, 16, S])
                            v.tensor_tensor(ypre[:], h_cs, cbt, OP.mult)
                            v.reduce_sum(ybuf[:, tloc, :], ypre[:], axis=AX.X)
                        v.last.then_inc(VE.h, 1)
                    VE.n += 1
                bank_rel[7] = (VE, VE.n)

            @blk.scalar
            def _(s):
                s.wait_ge(VE.h, VE.n)
                s.activation(out=ysq, in_=ybuf, func=AF.Square)
                s.last.then_inc(AC.h, 1)
            AC.n += 1

            @blk.vector
            def _(v):
                v.wait_ge(AC.h, AC.n)
                v.reduce_sum(yds[:, 0:B], ybuf, axis=AX.X).then_inc(VE.h, 1)
                v.reduce_sum(yds[:, B:2 * B], ysq, axis=AX.X).then_inc(VE.h, 1)
                VE.n += 2

            @blk.tensor
            def _(t):
                t.wait_ge(VE.h, VE.n)
                te_take(t, 5)
                t.matmul(pb[5][0:1, 0:2 * B], onesc_sb[:], yds[:], start=True, stop=True)
                t.last.then_inc(TE.h, 1)
            TE.n += 1

            @blk.vector
            def _(v):
                v.wait_ge(TE.h, TE.n)
                mu = strow_sb.ap()[:, 0:B]
                m2 = strow_sb.ap()[:, B:2 * B]
                var = strow_sb.ap()[:, 2 * B:3 * B]
                v.tensor_scalar(mu, pb[5][0:1, 0:B], 1.0 / DI, None, OP.mult).then_inc(VE.h, 1)
                v.tensor_scalar(m2, pb[5][0:1, B:2 * B], 1.0 / DI, None, OP.mult).then_inc(VE.h, 1)
                VE.n += 2
                v.wait_ge(VE.h, VE.n)
                v.tensor_tensor(var, mu, mu, OP.mult).then_inc(VE.h, 1)
                VE.n += 1
                v.wait_ge(VE.h, VE.n)
                v.tensor_tensor(var, m2, var, OP.subtract).then_inc(VE.h, 1)
                VE.n += 1
            bank_rel[5] = (VE, VE.n)

            @blk.scalar
            def _(s):
                s.wait_ge(VE.h, VE.n)
                var = strow_sb.ap()[:, 2 * B:3 * B]
                lnv = strow_sb.ap()[:, B:2 * B]
                rstd = strow_sb.ap()[:, 3 * B:4 * B]
                s.activation(out=lnv, in_=var, func=AF.Ln, bias=1e-5).then_inc(AC.h, 1)
                AC.n += 1
                s.wait_ge(AC.h, AC.n)
                s.activation(out=rstd, in_=lnv, func=AF.Exp, scale=-0.5).then_inc(AC.h, 1)
                AC.n += 1

            @blk.tensor
            def _(t):
                t.wait_ge(AC.h, AC.n)
                te_take(t, 5)
                t.matmul(pb[5][:, 0:B], onesr_sb[:], strow_sb.ap()[:, 0:B], start=True, stop=True)
                t.matmul(pb[5][:, B:2 * B], onesr_sb[:], strow_sb.ap()[:, 3 * B:4 * B],
                         start=True, stop=True)
                t.last.then_inc(TE.h, 1)
            TE.n += 1

            @blk.vector
            def _(v):
                v.wait_ge(TE.h, TE.n)
                mub = pb[5].ap()[:, 0:B][:, :, None].broadcast_to([128, B, 16])
                rsb = pb[5].ap()[:, B:2 * B][:, :, None].broadcast_to([128, B, 16])
                v.tensor_tensor(yln[:], ybuf[:], mub, OP.subtract).then_inc(VE.h, 1)
                VE.n += 1
                v.wait_ge(VE.h, VE.n)
                v.tensor_tensor(ybuf[:], yln[:], rsb, OP.mult).then_inc(VE.h, 1)
                VE.n += 1
                v.wait_ge(VE.h, VE.n)
                for dc in range(16):
                    v.tensor_scalar(yln[:, :, dc], ybuf[:, :, dc], lng_g.ap()[:, dc:dc + 1],
                                    lnb_g.ap()[:, dc:dc + 1], OP.mult, OP.add).then_inc(VE.h, 1)
                    VE.n += 1
            bank_rel[5] = (VE, VE.n)

            @blk.vector
            def _(v):
                v.wait_ge(VE.h, VE.n)
                for dc in range(16):
                    v.scalar_tensor_tensor(xc_r[:, dc, :], u_r[:, dc, :], D_g.ap()[:, dc:dc + 1],
                                           yln[:, :, dc], OP.mult, OP.add).then_inc(VE.h, 1)
                    VE.n += 1

            @blk.gpsimd
            def _(g):
                g.wait_ge(VE.h, VE.n)
                g.tensor_tensor(y2bf[:], xc_r[:], zs_r[:], OP.mult)
                g.last.then_inc(PO.h, 1)
            PO.n += 1

            opch = [(half, kc) for half in range(2) for kc in range(16)]
            op_te_counts = {}
            for i, (half, kc) in enumerate(opch):
                @blk.sync
                def _(sync, i=i, half=half, kc=kc):
                    if i >= 6 and (i - 6) in op_te_counts:
                        sync.wait_ge(TE.h, op_te_counts[i - 6])
                    sync.dma_start(out=opw_c[i % 6],
                                   in_=opw_bf[kc * 128:(kc + 1) * 128,
                                              half * 512:(half + 1) * 512]).then_inc(LD.h, 16)
                LD.n += 16

                @blk.tensor
                def _(t, i=i, half=half, kc=kc):
                    t.wait_ge(LD.h, LD.n)
                    if i == 0:
                        t.wait_ge(PO.h, PO.n)
                        te_take(t, 0)
                        te_take(t, 1)
                    t.matmul(pb[half][:, 0:512], y2bf[:, kc, :], opw_c[i % 6],
                             start=(kc == 0), stop=(kc == 15))
                    t.last.then_inc(TE.h, 1)
                TE.n += 1
                op_te_counts[i] = TE.n

            @blk.scalar
            def _(s):
                s.wait_ge(TE.h, TE.n)
                s.wait_ge(STm.h, STm.n)
                s.activation(out=s_tile[:, 0:512], in_=pb[0][:, 0:512], func=AF.Copy).then_inc(AC.h, 1)
                s.activation(out=s_tile[:, 512:1024], in_=pb[1][:, 0:512], func=AF.Copy).then_inc(AC.h, 1)
                AC.n += 2
            bank_rel[0] = (AC, AC.n)
            bank_rel[1] = (AC, AC.n)

            @blk.sync
            def _(sync, tb=tb):
                sync.wait_ge(AC.h, AC.n)
                sync.dma_start(out=s_dram[tb * B:(tb + 1) * B, :], in_=s_tile).then_inc(STm.h, 16)
            STm.n += 16

        for tb in range(NB if DO_PH3 else 0):
            @blk.sync
            def _(sync, tb=tb):
                wait_all(sync)
                sync.dma_start(out=p3x, in_=x_in[tb * B:(tb + 1) * B, :]).then_inc(LD.h, 16)
                sync.dma_start(out=p3g, in_=g_dram[tb * B:(tb + 1) * B, :]).then_inc(LD.h, 16)
                sync.dma_start(out=p3s, in_=s_dram[tb * B:(tb + 1) * B, :]).then_inc(LD.h, 16)
            LD.n += 48

            @blk.vector
            def _(v):
                v.wait_ge(LD.h, LD.n)
                mu = small_sb.ap()[:, 0:1]
                v.reduce_sum(mu, p3g[:, None, :], axis=AX.X).then_inc(VE.h, 1)
                VE.n += 1
                v.wait_ge(VE.h, VE.n)
                v.tensor_scalar(mu, mu, 1.0 / D, None, OP.mult).then_inc(VE.h, 1)
                VE.n += 1
                v.wait_ge(VE.h, VE.n)
                v.tensor_scalar(p3t, p3g, mu, None, OP.subtract).then_inc(VE.h, 1)
                VE.n += 1

            @blk.scalar
            def _(s):
                s.wait_ge(VE.h, VE.n)
                s.activation(out=p3o, in_=p3t, func=AF.Square)
                s.last.then_inc(AC.h, 1)
            AC.n += 1

            @blk.vector
            def _(v):
                v.wait_ge(AC.h, AC.n)
                var = small_sb.ap()[:, 2:3]
                v.reduce_sum(var, p3o[:, None, :], axis=AX.X).then_inc(VE.h, 1)
                VE.n += 1
                v.wait_ge(VE.h, VE.n)
                v.tensor_scalar(var, var, 1.0 / D, None, OP.mult).then_inc(VE.h, 1)
                VE.n += 1

            @blk.scalar
            def _(s):
                s.wait_ge(VE.h, VE.n)
                var = small_sb.ap()[:, 2:3]
                lnv = small_sb.ap()[:, 4:5]
                rstd = small_sb.ap()[:, 3:4]
                s.activation(out=lnv, in_=var, func=AF.Ln, bias=1e-5).then_inc(AC.h, 1)
                AC.n += 1
                s.wait_ge(AC.h, AC.n)
                s.activation(out=rstd, in_=lnv, func=AF.Exp, scale=-0.5).then_inc(AC.h, 1)
                AC.n += 1

            @blk.vector
            def _(v):
                v.wait_ge(AC.h, AC.n)
                rstd = small_sb.ap()[:, 3:4]
                v.tensor_scalar(p3o, p3t, rstd, None, OP.mult).then_inc(VE.h, 1)
                VE.n += 1
                v.wait_ge(VE.h, VE.n)
                v.tensor_tensor(p3t, p3o, gam_sb[:], OP.mult).then_inc(VE.h, 1)
                VE.n += 1
                v.wait_ge(VE.h, VE.n)
                v.tensor_tensor(p3o, p3t, bet_sb[:], OP.add).then_inc(VE.h, 1)
                VE.n += 1

            @blk.scalar
            def _(s):
                s.wait_ge(VE.h, VE.n)
                s.activation(out=p3t, in_=p3o, func=AF.Silu)
                s.last.then_inc(AC.h, 1)
            AC.n += 1

            @blk.vector
            def _(v):
                v.wait_ge(AC.h, AC.n)
                v.tensor_tensor(p3o, p3s, p3t, OP.mult).then_inc(VE.h, 1)
                VE.n += 1
                v.wait_ge(VE.h, VE.n)
                v.tensor_tensor(p3o, p3o, p3x, OP.add).then_inc(VE.h, 1)
                VE.n += 1

            @blk.sync
            def _(sync, tb=tb):
                sync.wait_ge(VE.h, VE.n)
                sync.dma_start(out=out[tb * B:(tb + 1) * B, :], in_=p3o).then_inc(STm.h, 16)
            STm.n += 16

        @blk.sync
        def _(sync):
            sync.wait_ge(STm.h, STm.n)

    return nc


def _prep_inputs(inputs):
    f = np.float32
    ip = np.ascontiguousarray(inputs["L0_attn_in_proj_w"].T).astype(f)
    op = np.ascontiguousarray(inputs["L0_attn_out_proj_w"].T).astype(f)
    xp = inputs["L0_attn_x_proj_w"]
    xpw = np.zeros((DI, 80), f)
    xpw[:, 0] = xp[0]
    xpw[:, 32:48] = xp[1:17].T
    xpw[:, 64:80] = xp[17:33].T
    dtwr = np.ascontiguousarray(inputs["L0_attn_dt_proj_w"][:, 0][None, :]).astype(f)
    gw = np.ascontiguousarray(inputs["L0_ffn_gate_w"].T).astype(f)

    def grid(vec):
        return np.ascontiguousarray(np.asarray(vec, f).reshape(16, 128).T)

    convw = np.ascontiguousarray(
        np.asarray(inputs["L0_attn_conv_w"], f)[:, 0, :].reshape(16, 128, 4).transpose(1, 0, 2))

    sel = np.zeros((S, S * 128), f)
    for s in range(S):
        sel[s, s * 128:(s + 1) * 128] = 1.0

    cconst = np.zeros((128, 8), f)
    cconst[:, 0] = 1.0
    cconst[:, 1] = 1e-5
    cconst[:, 2] = -0.5
    cconst[:, 3] = -1.0

    common = {
        "ipwT": ip, "opwT": op, "xpw": xpw, "dtw": dtwr, "gwT": gw,
        "w1": np.ascontiguousarray(inputs["L0_ffn_expert_w1"], f),
        "w2": np.ascontiguousarray(inputs["L0_ffn_expert_w2"], f),
        "g_convb": grid(inputs["L0_attn_conv_b"]),
        "g_dtb": grid(inputs["L0_attn_dt_proj_b"]),
        "g_D": grid(inputs["L0_attn_D"]),
        "g_lng": grid(inputs["L0_attn_ln_g"]),
        "g_lnb": grid(inputs["L0_attn_ln_b"]),
        "g_convw": convw,
        "alog": np.asarray(inputs["L0_attn_A_log"], f)[None, :],
        "gamma_b": np.ascontiguousarray(np.broadcast_to(np.asarray(inputs["L0_ffn_ln_gamma"], f), (128, D))),
        "beta_b": np.ascontiguousarray(np.broadcast_to(np.asarray(inputs["L0_ffn_ln_beta"], f), (128, D))),
        "ones_col": np.ones((128, 1), f),
        "ones_row": np.ones((1, 128), f),
        "sel16": sel,
        "cconst": cconst,
        "ident_in": np.eye(128, dtype=f),
    }
    x = np.asarray(inputs["L0_attn_x"], f)
    in_maps = []
    for b in range(NCORE):
        m = dict(common)
        m["x"] = np.ascontiguousarray(x[b])
        in_maps.append(m)
    return in_maps


def kernel(**inputs):
    in_maps = _prep_inputs(inputs)
    nc = build(debug=False)
    res = run_bass_kernel_spmd(nc, in_maps, core_ids=list(range(NCORE)))
    outs = [res.results[b]["out"] for b in range(NCORE)]
    return np.stack(outs, axis=0).astype(np.float32)


# version 04
# speedup vs baseline: 1.5031x; 1.5031x over previous
import contextlib
import numpy as np
import ml_dtypes
import concourse.bass as bass
import concourse.mybir as mybir
from concourse.bass_utils import run_bass_kernel_spmd

F32 = mybir.dt.float32
BF16 = mybir.dt.bfloat16
U32 = mybir.dt.uint32
U8 = mybir.dt.uint8
AF = mybir.ActivationFunctionType
OP = mybir.AluOpType
AX = mybir.AxisListType

BATCH, T, D, DI, S, E, H, NCORE = 8, 4096, 1024, 2048, 16, 8, 2048, 8
B = 128
NB = T // B
TC = 8
QC = 32
HRING = 2
TG = 512
NTG = T // TG
NTC = TG // 128
EXP_M20 = float(np.exp(np.float32(-20.0)))
DO_PH1 = True
DO_PH2 = True
DO_PH3 = True


class Sem:
    def __init__(self, es, nc, name):
        self.h = es.enter_context(nc.semaphore(name))
        self.n = 0


_NOTRACK = {"wait_ge", "nop", "engine_nop"}


class EngLast:

    def __init__(self, eng):
        object.__setattr__(self, "_e", eng)
        object.__setattr__(self, "last", None)

    def __getattr__(self, name):
        f = getattr(self._e, name)
        if callable(f) and name not in _NOTRACK and not name.startswith("_"):
            def wrap(*a, **k):
                r = f(*a, **k)
                if r is not None and hasattr(r, "then_inc"):
                    object.__setattr__(self, "last", r)
                return r
            return wrap
        return f


class WBlk:
    def __init__(self, blk):
        self._b = blk

    def _mk(self, name):
        dec = getattr(self._b, name)

        def deco(fn):
            def inner(eng):
                return fn(EngLast(eng))
            return dec(inner)
        return deco

    @property
    def vector(self):
        return self._mk("vector")

    @property
    def scalar(self):
        return self._mk("scalar")

    @property
    def tensor(self):
        return self._mk("tensor")

    @property
    def gpsimd(self):
        return self._mk("gpsimd")

    @property
    def sync(self):
        return self._mk("sync")


def build(debug=False):
    nc = bass.Bass("TRN2", target_bir_lowering=False)

    x_in = nc.dram_tensor("x", [T, D], F32, kind="ExternalInput").ap()
    ipwT = nc.dram_tensor("ipwT", [D, 2 * DI], F32, kind="ExternalInput").ap()
    opwT = nc.dram_tensor("opwT", [DI, D], F32, kind="ExternalInput").ap()
    xpw = nc.dram_tensor("xpw", [DI, 80], F32, kind="ExternalInput").ap()
    dtw = nc.dram_tensor("dtw", [1, DI], F32, kind="ExternalInput").ap()
    gwT = nc.dram_tensor("gwT", [D, E], F32, kind="ExternalInput").ap()
    w1 = nc.dram_tensor("w1", [E, D, H], F32, kind="ExternalInput").ap()
    w2 = nc.dram_tensor("w2", [E, H, D], F32, kind="ExternalInput").ap()
    g_convb = nc.dram_tensor("g_convb", [128, 16], F32, kind="ExternalInput").ap()
    g_dtb = nc.dram_tensor("g_dtb", [128, 16], F32, kind="ExternalInput").ap()
    g_D = nc.dram_tensor("g_D", [128, 16], F32, kind="ExternalInput").ap()
    g_lng = nc.dram_tensor("g_lng", [128, 16], F32, kind="ExternalInput").ap()
    g_lnb = nc.dram_tensor("g_lnb", [128, 16], F32, kind="ExternalInput").ap()
    g_convw = nc.dram_tensor("g_convw", [128, 16, 4], F32, kind="ExternalInput").ap()
    alog = nc.dram_tensor("alog", [1, S], F32, kind="ExternalInput").ap()
    gamma_b = nc.dram_tensor("gamma_b", [128, D], F32, kind="ExternalInput").ap()
    beta_b = nc.dram_tensor("beta_b", [128, D], F32, kind="ExternalInput").ap()
    ones_col = nc.dram_tensor("ones_col", [128, 1], F32, kind="ExternalInput").ap()
    ones_row = nc.dram_tensor("ones_row", [1, 128], F32, kind="ExternalInput").ap()
    sel16 = nc.dram_tensor("sel16", [S, S * 128], F32, kind="ExternalInput").ap()
    cconst = nc.dram_tensor("cconst", [128, 8], F32, kind="ExternalInput").ap()
    ident_in = nc.dram_tensor("ident_in", [128, 128], F32, kind="ExternalInput").ap()

    out = nc.dram_tensor("out", [T, D], F32, kind="ExternalOutput").ap()
    okind = "ExternalOutput" if debug else "Internal"
    xn_dram = nc.dram_tensor("xn_dram", [T, D], BF16, kind="Internal").ap()
    g_dram = nc.dram_tensor("g_dram", [T, D], F32, kind=okind).ap()
    s_dram = nc.dram_tensor("s_dram", [T, D], F32, kind=okind).ap()
    xn_f32 = nc.dram_tensor("xn_f32", [T, D], F32, kind=okind).ap() if debug else None
    opw_bf = nc.dram_tensor("opw_bf", [DI, D], BF16, kind="Internal").ap()

    es = contextlib.ExitStack()
    with es:
        ipw_sb = es.enter_context(nc.sbuf_tensor("ipw_sb", [128, 8, 2 * DI], BF16))
        convb_g = es.enter_context(nc.sbuf_tensor("convb_g", [128, 16], F32))
        dtb_g = es.enter_context(nc.sbuf_tensor("dtb_g", [128, 16], F32))
        D_g = es.enter_context(nc.sbuf_tensor("D_g", [128, 16], F32))
        lng_g = es.enter_context(nc.sbuf_tensor("lng_g", [128, 16], F32))
        lnb_g = es.enter_context(nc.sbuf_tensor("lnb_g", [128, 16], F32))
        convw_g = es.enter_context(nc.sbuf_tensor("convw_g", [128, 16, 4], F32))
        alog_sb = es.enter_context(nc.sbuf_tensor("alog_sb", [1, S], F32))
        arow_sb = es.enter_context(nc.sbuf_tensor("arow_sb", [1, S], F32))
        abc_sb = es.enter_context(nc.sbuf_tensor("abc_sb", [128, S], F32))
        gam_sb = es.enter_context(nc.sbuf_tensor("gam_sb", [128, D], F32))
        bet_sb = es.enter_context(nc.sbuf_tensor("bet_sb", [128, D], F32))
        onesc_sb = es.enter_context(nc.sbuf_tensor("onesc_sb", [128, 1], F32))
        onesr_sb = es.enter_context(nc.sbuf_tensor("onesr_sb", [1, 128], F32))
        sel_sb = es.enter_context(nc.sbuf_tensor("sel_sb", [S, S * 128], F32))
        cc_sb = es.enter_context(nc.sbuf_tensor("cc_sb", [128, 8], F32))
        xpw_sb = es.enter_context(nc.sbuf_tensor("xpw_sb", [128, 16, 80], F32))
        dtw_sb = es.enter_context(nc.sbuf_tensor("dtw_sb", [1, DI], F32))
        gw_sb = es.enter_context(nc.sbuf_tensor("gw_sb", [128, 8, E], F32))
        ident_sb = es.enter_context(nc.sbuf_tensor("ident_sb", [128, 128], F32))
        mtv_sb = es.enter_context(nc.sbuf_tensor("mtv_sb", [128, NB, E], F32))
        dtraw_sb = es.enter_context(nc.sbuf_tensor("dtraw_sb", [1, B], F32))
        strow_sb = es.enter_context(nc.sbuf_tensor("strow_sb", [1, 4 * B], F32))
        BC_sb = es.enter_context(nc.sbuf_tensor("BC_sb", [16, 2 * B], F32))
        small_sb = es.enter_context(nc.sbuf_tensor("small_sb", [128, 32], F32))
        smallu_sb = es.enter_context(nc.sbuf_tensor("smallu_sb", [128, 8], U32))

        ARENA_BYTES = 108 * 1024
        arena = es.enter_context(nc.sbuf_tensor("arena", [128, ARENA_BYTES], U8))
        aap = arena.ap()

        class Carver:
            def __init__(self):
                self.off = 0

            def take(self, shape, dtype, name=""):
                n = int(np.prod(shape[1:]))
                nbytes = n * mybir.dt.size(dtype)
                off = (self.off + 31) // 32 * 32
                assert off + nbytes <= ARENA_BYTES, f"arena overflow at {name}: {off}+{nbytes}"
                self.off = off + nbytes
                ap = aap[:, off:off + nbytes].bitcast(dtype)
                if len(shape) > 2:
                    pat = "p (" + " ".join(f"d{i}" for i in range(1, len(shape))) + ") -> p " + \
                          " ".join(f"d{i}" for i in range(1, len(shape)))
                    kw = {f"d{i}": shape[i] for i in range(1, len(shape) - 1)}
                    ap = ap.rearrange(pat, **kw)
                return ap

        c = Carver()
        _xt = c.take([128, D], F32, "xt0")
        xtile = [_xt, _xt]
        psq = c.take([128, D], F32, "psq")
        xnf32 = c.take([128, D], F32, "xnf32")
        xnT = psq.rearrange("p (a b) -> p a b", a=8)
        xn_out = c.take([128, D], BF16, "xn_out")
        gate_e = c.take([128, E], F32, "gate_e")
        _xnt = c.take([128, 8, TG], BF16, "xnt0")
        xnt = [_xnt, _xnt]
        w1_sb = c.take([128, 8, H], BF16, "w1_sb")
        w2_sb = c.take([128, 16, D], BF16, "w2_sb")
        heg = c.take([128, 16, TG], BF16, "heg")
        gtile = c.take([128, D], F32, "gtile")
        gate_sb = c.take([128, E], F32, "gate_sb")
        ph01_high = c.off

        c = Carver()
        xnp = c.take([128, 8, B], BF16, "xnp")
        x_in0 = c.take([128, 16, B + 3], F32, "x_in0")
        xc_r = c.take([128, 16, B], F32, "xc_r")
        u_r = c.take([128, 16, B], F32, "u_r")
        zs_r = c.take([128, 16, B], F32, "zs_r")
        dt_r = c.take([128, 16, B], F32, "dt_r")
        dtu_r = c.take([128, 16, B], F32, "dtu_r")
        dA_c = c.take([128, S, 16, TC], F32, "dA_c")
        dBu_c = c.take([128, S, 16, TC], F32, "dBu_c")
        hring = c.take([128, HRING, 256], F32, "hring")
        v1_s = c.take([128, S, 16], F32, "v1")
        v2_s = c.take([128, 256], F32, "v2")
        ypre = c.take([128, 16, S], F32, "ypre")
        ybuf = c.take([128, B, 16], F32, "ybuf")
        ysq = c.take([128, B, 16], F32, "ysq")
        yds = c.take([128, 2 * B], F32, "yds")
        yln = ysq
        Bb_sb = c.take([128, S, QC], F32, "Bb_sb")
        y2bf = c.take([128, 16, B], BF16, "y2bf")
        opw_c = [c.take([128, 512], BF16, f"opw{i}") for i in range(6)]
        s_tile = c.take([128, D], F32, "s_tile")
        spt = [c.take([128, B], F32, f"spt{i}") for i in range(3)]
        ph2_high = c.off

        c = Carver()
        p3x = c.take([128, D], F32, "p3x")
        p3g = c.take([128, D], F32, "p3g")
        p3s = c.take([128, D], F32, "p3s")
        p3t = c.take([128, D], F32, "p3t")
        p3o = c.take([128, D], F32, "p3o")
        ph3_high = c.off

        if debug:
            print(f"[build] arena usage: ph01={ph01_high} ph2={ph2_high} ph3={ph3_high} of {ARENA_BYTES}")
            print(f"[build] sbuf remaining: {nc.sbuf_bytes_remaining}")

        pb = [es.enter_context(nc.psum_tensor(f"pb{i}", [128, 512], F32)) for i in range(8)]

        LD = Sem(es, nc, "LD")
        LDg = Sem(es, nc, "LDg")
        STm = Sem(es, nc, "STm")
        STg = Sem(es, nc, "STg")
        TE = Sem(es, nc, "TE")
        AC = Sem(es, nc, "AC")
        VE = Sem(es, nc, "VE")
        PO = Sem(es, nc, "PO")
        allsems = [LD, LDg, STm, STg, TE, AC, VE, PO]

        blk = WBlk(es.enter_context(nc.Block()))

        def wait_all(eng):
            for sm in allsems:
                if sm.n > 0:
                    eng.wait_ge(sm.h, sm.n)

        bank_rel = [None] * 8

        def te_take(t, i):
            if bank_rel[i] is not None:
                sm, n = bank_rel[i]
                t.wait_ge(sm.h, n)
                bank_rel[i] = None

        @blk.sync
        def _(sync):
            for dst, src in [(convb_g, g_convb), (dtb_g, g_dtb), (D_g, g_D), (lng_g, g_lng),
                             (lnb_g, g_lnb), (alog_sb, alog), (gam_sb, gamma_b), (bet_sb, beta_b),
                             (onesc_sb, ones_col), (onesr_sb, ones_row), (sel_sb, sel16),
                             (cc_sb, cconst), (dtw_sb, dtw)]:
                sync.dma_start(out=dst[:], in_=src).then_inc(LD.h, 16)
                LD.n += 16
            sync.dma_start(out=convw_g[:], in_=g_convw).then_inc(LD.h, 16)
            LD.n += 16
            sync.dma_start(out=xpw_sb[:], in_=xpw.rearrange("(c p) m -> p c m", p=128)).then_inc(LD.h, 16)
            LD.n += 16

        @blk.gpsimd
        def _(g):
            g.dma_start(out=gw_sb[:], in_=gwT.rearrange("(c p) m -> p c m", p=128)).then_inc(LDg.h, 16)
            LDg.n += 16
            g.dma_start(out=ident_sb[:], in_=ident_in).then_inc(LDg.h, 16)
            LDg.n += 16
            g.dma_start(out=opw_bf, in_=opwT).then_inc(LDg.h, 16)
            LDg.n += 16

        nc.const_aps.register(F32, 1.0, cc_sb.ap()[:, 0:1])
        nc.const_aps.register(F32, 1e-5, cc_sb.ap()[:, 1:2])
        nc.const_aps.register(F32, -0.5, cc_sb.ap()[:, 2:3])
        nc.const_aps.register(F32, -1.0, cc_sb.ap()[:, 3:4])

        @blk.vector
        def _(v):
            wait_all(v)
            v.tensor_scalar(arow_sb[:], alog_sb[:], 5.0, None, OP.min)
            v.last.then_inc(VE.h, 1)
        VE.n += 1

        @blk.scalar
        def _(s):
            s.wait_ge(VE.h, VE.n)
            s.activation(out=alog_sb[:], in_=arow_sb[:], func=AF.Exp)
            s.last.then_inc(AC.h, 1)
        AC.n += 1

        @blk.tensor
        def _(t):
            t.wait_ge(AC.h, AC.n)
            t.matmul(pb[5][:, 0:S], onesr_sb[:], alog_sb[:], start=True, stop=True)
            t.last.then_inc(TE.h, 1)
        TE.n += 1

        @blk.vector
        def _(v):
            v.wait_ge(TE.h, TE.n)
            v.tensor_scalar(abc_sb[:], pb[5][:, 0:S], -1.0, None, OP.mult)
            v.last.then_inc(VE.h, 1)
        VE.n += 1
        bank_rel[5] = (VE, VE.n)

        for tb in range(NB):
            buf = xtile[tb % 2]

            @blk.sync
            def _(sync, tb=tb, buf=buf):
                wait_all(sync)
                sync.dma_start(out=buf, in_=x_in[tb * B:(tb + 1) * B, :]).then_inc(LD.h, 16)
            LD.n += 16

            @blk.scalar
            def _(s, buf=buf):
                s.wait_ge(LD.h, LD.n)
                s.activation(out=psq, in_=buf, func=AF.Square)
                s.last.then_inc(AC.h, 1)
            AC.n += 1

            @blk.vector
            def _(v, buf=buf):
                v.wait_ge(AC.h, AC.n)
                mu = small_sb.ap()[:, 0:1]
                m2 = small_sb.ap()[:, 1:2]
                nvar = small_sb.ap()[:, 2:3]
                v.reduce_sum(mu, buf[:, None, :], axis=AX.X).then_inc(VE.h, 1)
                v.reduce_sum(m2, psq[:, None, :], axis=AX.X).then_inc(VE.h, 1)
                VE.n += 2
                v.wait_ge(VE.h, VE.n)
                v.tensor_scalar(mu, mu, 1.0 / D, None, OP.mult).then_inc(VE.h, 1)
                v.tensor_scalar(m2, m2, 1.0 / D, None, OP.mult).then_inc(VE.h, 1)
                VE.n += 2
                v.wait_ge(VE.h, VE.n)
                v.scalar_tensor_tensor(nvar, mu, mu, m2, OP.mult, OP.subtract).then_inc(VE.h, 1)
                VE.n += 1

            @blk.scalar
            def _(s):
                s.wait_ge(VE.h, VE.n)
                nvar = small_sb.ap()[:, 2:3]
                lnv = small_sb.ap()[:, 17:18]
                rstd = small_sb.ap()[:, 3:4]
                s.activation(out=lnv, in_=nvar, func=AF.Ln, scale=-1.0, bias=1e-5).then_inc(AC.h, 1)
                AC.n += 1
                s.wait_ge(AC.h, AC.n)
                s.activation(out=rstd, in_=lnv, func=AF.Exp, scale=-0.5).then_inc(AC.h, 1)
                AC.n += 1

            @blk.vector
            def _(v, buf=buf):
                v.wait_ge(AC.h, AC.n)
                v.wait_ge(STm.h, STm.n)
                v.tensor_scalar(xnf32, buf, small_sb.ap()[:, 0:1], small_sb.ap()[:, 3:4],
                                OP.subtract, OP.mult).then_inc(VE.h, 1)
                VE.n += 1
                v.wait_ge(VE.h, VE.n)
                v.tensor_copy(xn_out, xnf32).then_inc(VE.h, 1)
                VE.n += 1

            @blk.sync
            def _(sync, tb=tb):
                sync.wait_ge(VE.h, VE.n)
                sync.dma_start(out=xn_dram[tb * B:(tb + 1) * B, :], in_=xn_out).then_inc(STm.h, 16)
            STm.n += 16

            for dc in range(8):
                bk = dc % 2

                @blk.tensor
                def _(t, dc=dc, bk=bk):
                    if dc == 0:
                        t.wait_ge(VE.h, VE.n)
                        t.wait_ge(LDg.h, LDg.n)
                    te_take(t, bk)
                    t.transpose(pb[bk][:, 0:128], xnf32[:, dc * 128:(dc + 1) * 128], ident_sb[:])
                    t.last.then_inc(TE.h, 1)
                TE.n += 1

                @blk.scalar
                def _(s, dc=dc, bk=bk):
                    s.wait_ge(TE.h, TE.n)
                    s.activation(out=xnT[:, dc, :], in_=pb[bk][:, 0:128], func=AF.Copy)
                    s.last.then_inc(AC.h, 1)
                AC.n += 1
                bank_rel[bk] = (AC, AC.n)

            @blk.tensor
            def _(t):
                t.wait_ge(AC.h, AC.n)
                te_take(t, 5)
                for kc in range(8):
                    t.matmul(pb[5][:, 0:E], xnT[:, kc, :], gw_sb.ap()[:, kc, :],
                             start=(kc == 0), stop=(kc == 7))
                t.last.then_inc(TE.h, 1)
            TE.n += 1

            @blk.vector
            def _(v):
                v.wait_ge(TE.h, TE.n)
                mx0 = small_sb.ap()[:, 4:5]
                v.tensor_copy(gate_sb, pb[5][:, 0:E]).then_inc(VE.h, 1)
                VE.n += 1
                v.wait_ge(VE.h, VE.n)
                v.reduce_max(mx0, gate_sb[:, None, :], axis=AX.X).then_inc(VE.h, 1)
                VE.n += 1
                v.wait_ge(VE.h, VE.n)
                v.tensor_scalar(small_sb.ap()[:, 12:13], mx0, -1.0, None, OP.mult).then_inc(VE.h, 1)
                VE.n += 1
            bank_rel[5] = (VE, VE.n)

            @blk.scalar
            def _(s):
                s.wait_ge(VE.h, VE.n)
                s.activation(out=gate_e, in_=gate_sb, func=AF.Exp, bias=small_sb.ap()[:, 12:13])
                s.last.then_inc(AC.h, 1)
            AC.n += 1

            @blk.vector
            def _(v, tb=tb):
                v.wait_ge(AC.h, AC.n)
                ssum = small_sb.ap()[:, 13:14]
                tv = small_sb.ap()[:, 14:15]
                mx0 = small_sb.ap()[:, 4:5]
                v.reduce_sum(ssum, gate_e[:, None, :], axis=AX.X).then_inc(VE.h, 1)
                VE.n += 1
                v.wait_ge(VE.h, VE.n)
                v.tensor_scalar(tv, ssum, 1e-8, 1.0, OP.mult, OP.add).then_inc(VE.h, 1)
                VE.n += 1
                v.wait_ge(VE.h, VE.n)
                v.reciprocal(tv, tv).then_inc(VE.h, 1)
                VE.n += 1
                v.wait_ge(VE.h, VE.n)
                for e in range(E):
                    v.scalar_tensor_tensor(mtv_sb.ap()[:, tb, e:e + 1], gate_sb[:, e:e + 1],
                                           mx0, tv, OP.is_equal, OP.mult).then_inc(VE.h, 1)
                    VE.n += 1

        if debug:
            @blk.gpsimd
            def _(g):
                wait_all(g)
                g.dma_start(out=xn_f32, in_=xn_dram).then_inc(STg.h, 16)
            STg.n += 16

        _ph1 = DO_PH1
        def load_xnt(tg, buf):
            @blk.sync
            def _(sync, tg=tg, buf=buf):
                wait_all(sync)
                for dc in range(8):
                    sync.dma_start(
                        out=buf[:, dc, :],
                        in_=xn_dram[tg * TG:(tg + 1) * TG, dc * 128:(dc + 1) * 128],
                        transpose=True,
                    ).then_inc(LD.h, 16)
            LD.n += 8 * 16

        for e in range(E if DO_PH1 else 0):
            @blk.gpsimd
            def _(g, e=e):
                wait_all(g)
                g.dma_start(out=w1_sb, in_=w1[e].rearrange("(c p) h -> p c h", p=128)).then_inc(LDg.h, 16)
                g.dma_start(out=w2_sb, in_=w2[e].rearrange("(c p) h -> p c h", p=128)).then_inc(LDg.h, 16)
            LDg.n += 32

            for tg in range(NTG):
                buf = xnt[tg % 2]
                load_xnt(tg, buf)

                for hc in range(16):
                    bk = hc % 2

                    @blk.tensor
                    def _(t, buf=buf, hc=hc, bk=bk):
                        if hc == 0:
                            t.wait_ge(LD.h, LD.n)
                            t.wait_ge(LDg.h, LDg.n)
                        te_take(t, bk)
                        for kc in range(8):
                            t.matmul(pb[bk][:, 0:TG], w1_sb[:, kc, hc * 128:(hc + 1) * 128],
                                     buf[:, kc, 0:TG], start=(kc == 0), stop=(kc == 7))
                        t.last.then_inc(TE.h, 1)
                    TE.n += 1

                    @blk.scalar
                    def _(s, hc=hc, bk=bk):
                        s.wait_ge(TE.h, TE.n)
                        s.activation(out=heg[:, hc, :], in_=pb[bk][:, 0:TG], func=AF.Gelu)
                        s.last.then_inc(AC.h, 1)
                    AC.n += 1
                    bank_rel[bk] = (AC, AC.n)

                for tc in range(NTC):
                    tbi = tg * NTC + tc

                    @blk.tensor
                    def _(t, tc=tc):
                        t.wait_ge(AC.h, AC.n)
                        te_take(t, 2)
                        te_take(t, 3)
                        for half in range(2):
                            for kc in range(16):
                                t.matmul(pb[2 + half][:, 0:512],
                                         heg[:, kc, tc * 128:(tc + 1) * 128],
                                         w2_sb[:, kc, half * 512:(half + 1) * 512],
                                         start=(kc == 0), stop=(kc == 15))
                        t.last.then_inc(TE.h, 1)
                    TE.n += 1

                    @blk.vector
                    def _(v, e=e, tbi=tbi):
                        v.wait_ge(TE.h, TE.n)
                        v.wait_ge(STg.h, STg.n)
                        mtv = mtv_sb.ap()[:, tbi, e:e + 1]
                        v.tensor_scalar(gtile[:, 0:512], pb[2][:, 0:512], mtv, None,
                                        OP.mult).then_inc(VE.h, 1)
                        v.tensor_scalar(gtile[:, 512:1024], pb[3][:, 0:512], mtv, None,
                                        OP.mult).then_inc(VE.h, 1)
                        VE.n += 2
                    bank_rel[2] = (VE, VE.n)
                    bank_rel[3] = (VE, VE.n)

                    @blk.gpsimd
                    def _(g, e=e, tbi=tbi):
                        g.wait_ge(VE.h, VE.n)
                        if e == 0:
                            g.dma_start(out=g_dram[tbi * B:(tbi + 1) * B, :], in_=gtile).then_inc(STg.h, 16)
                        else:
                            g.dma_start(out=g_dram[tbi * B:(tbi + 1) * B, :], in_=gtile,
                                        accum_op=OP.add).then_inc(STg.h, 16)
                    STg.n += 16

        @blk.gpsimd
        def _(g):
            wait_all(g)
            g.dma_start(out=ipw_sb[:], in_=ipwT.rearrange("(c p) m -> p c m", p=128)).then_inc(LDg.h, 16)
        LDg.n += 16

        @blk.vector
        def _(v):
            wait_all(v)
            v.memset(hring[:, HRING - 1, :], 0.0)
            v.memset(x_in0[:, :, 0:3], 0.0)
            v.last.then_inc(VE.h, 1)
        VE.n += 1

        for tb in range(NB if DO_PH2 else 0):
            t0g = tb * B

            @blk.sync
            def _(sync, t0g=t0g):
                wait_all(sync)
                for dc in range(8):
                    sync.dma_start(
                        out=xnp[:, dc, :],
                        in_=xn_dram[t0g:t0g + B, dc * 128:(dc + 1) * 128],
                        transpose=True,
                    ).then_inc(LD.h, 16)
            LD.n += 8 * 16

            for mc in range(32):
                bk = mc % 4

                @blk.tensor
                def _(t, mc=mc, bk=bk):
                    if mc == 0:
                        wait_all(t)
                    te_take(t, bk)
                    for kc in range(8):
                        t.matmul(pb[bk][:, 0:B], ipw_sb[:, kc, mc * 128:(mc + 1) * 128],
                                 xnp[:, kc, :], start=(kc == 0), stop=(kc == 7))
                    t.last.then_inc(TE.h, 1)
                TE.n += 1

                @blk.scalar
                def _(s, mc=mc, bk=bk):
                    s.wait_ge(TE.h, TE.n)
                    if mc < 16:
                        s.activation(out=x_in0[:, mc, 3:3 + B], in_=pb[bk][:, 0:B], func=AF.Copy)
                    else:
                        s.activation(out=zs_r[:, mc - 16, :], in_=pb[bk][:, 0:B], func=AF.Silu)
                    s.last.then_inc(AC.h, 1)
                AC.n += 1
                bank_rel[bk] = (AC, AC.n)

            @blk.vector
            def _(v):
                v.wait_ge(AC.h, AC.n)
                for dc in range(16):
                    wcol = convw_g.ap()[:, dc, :]
                    v.tensor_scalar(xc_r[:, dc, :], x_in0[:, dc, 0:B], wcol[:, 0:1],
                                    convb_g.ap()[:, dc:dc + 1], OP.mult, OP.add)
                for k in range(1, 4):
                    for dc in range(16):
                        wcol = convw_g.ap()[:, dc, :]
                        r = v.scalar_tensor_tensor(xc_r[:, dc, :], x_in0[:, dc, k:k + B],
                                                   wcol[:, k:k + 1], xc_r[:, dc, :], OP.mult, OP.add)
                        if k == 3:
                            r.then_inc(VE.h, 1)
                            VE.n += 1
                v.tensor_copy(x_in0[:, :, 0:3], x_in0[:, :, B:B + 3]).then_inc(VE.h, 1)
                VE.n += 1

            @blk.scalar
            def _(s):
                s.wait_ge(VE.h, VE.n)
                s.activation(out=u_r[:], in_=xc_r[:], func=AF.Silu)
                s.last.then_inc(AC.h, 1)
            AC.n += 1

            @blk.tensor
            def _(t):
                t.wait_ge(AC.h, AC.n)
                te_take(t, 4)
                for kc in range(16):
                    t.matmul(pb[4][0:80, 0:B], xpw_sb[:, kc, :], u_r[:, kc, :],
                             start=(kc == 0), stop=(kc == 15))
                t.last.then_inc(TE.h, 1)
            TE.n += 1

            @blk.vector
            def _(v):
                v.wait_ge(TE.h, TE.n)
                v.tensor_copy(dtraw_sb[:], pb[4][0:1, 0:B]).then_inc(VE.h, 1)
                v.tensor_copy(BC_sb.ap()[:, 0:B], pb[4][32:48, 0:B]).then_inc(VE.h, 1)
                v.tensor_copy(BC_sb.ap()[:, B:2 * B], pb[4][64:80, 0:B]).then_inc(VE.h, 1)
                VE.n += 3
            bank_rel[4] = (VE, VE.n)

            for bi in range(4):
                @blk.tensor
                def _(t, bi=bi):
                    if bi == 0:
                        t.wait_ge(VE.h, VE.n)
                    te_take(t, 4 + bi)
                    for j in range(4):
                        dc = bi * 4 + j
                        t.matmul(pb[4 + bi][:, j * B:(j + 1) * B],
                                 dtw_sb.ap()[:, dc * 128:(dc + 1) * 128],
                                 dtraw_sb[:], start=True, stop=True)
                    t.last.then_inc(TE.h, 1)
                TE.n += 1

            @blk.scalar
            def _(s):
                s.wait_ge(TE.h, TE.n)
                s.wait_ge(PO.h, PO.n)
                for dc in range(16):
                    bi, j = dc // 4, dc % 4
                    s.activation(out=dtu_r[:, dc, :], in_=pb[4 + bi][:, j * B:(j + 1) * B],
                                 func=AF.Exp, bias=dtb_g.ap()[:, dc:dc + 1]).then_inc(AC.h, 1)
                    AC.n += 1
                s.wait_ge(AC.h, AC.n)
                for dc in range(16):
                    s.activation(out=dt_r[:, dc, :], in_=dtu_r[:, dc, :],
                                 func=AF.Ln, bias=1.0).then_inc(AC.h, 1)
                    AC.n += 1
            for bi in range(4):
                bank_rel[4 + bi] = (AC, AC.n)

            @blk.vector
            def _(v):
                v.wait_ge(AC.h, AC.n)
                v.tensor_scalar(dt_r[:], dt_r[:], 10.0, 1e-4,
                                OP.min, OP.max).then_inc(VE.h, 1)
                VE.n += 1

            @blk.gpsimd
            def _(g):
                g.wait_ge(VE.h, VE.n)
                g.wait_ge(AC.h, AC.n)
                g.tensor_tensor(dtu_r[:], dt_r[:], u_r[:], OP.mult)
                g.last.then_inc(PO.h, 1)
            PO.n += 1

            for q in range(4):
                @blk.tensor
                def _(t, q=q):
                    if q == 0:
                        t.wait_ge(PO.h, PO.n)
                    te_take(t, 6)
                    te_take(t, 7)
                    for si in range(S):
                        t.matmul(pb[6][:, si * QC:(si + 1) * QC],
                                 sel_sb.ap()[:, si * 128:(si + 1) * 128],
                                 BC_sb.ap()[:, q * QC:(q + 1) * QC], start=True, stop=True)
                        t.matmul(pb[7][:, si * QC:(si + 1) * QC],
                                 sel_sb.ap()[:, si * 128:(si + 1) * 128],
                                 BC_sb.ap()[:, B + q * QC:B + (q + 1) * QC], start=True, stop=True)
                    t.last.then_inc(TE.h, 1)
                TE.n += 1

                @blk.scalar
                def _(s):
                    s.wait_ge(TE.h, TE.n)
                    s.wait_ge(PO.h, PO.n)
                    s.activation(out=Bb_sb.rearrange("p a b -> p (a b)"),
                                 in_=pb[6].ap()[:, 0:S * QC], func=AF.Copy)
                    s.last.then_inc(AC.h, 1)
                AC.n += 1
                bank_rel[6] = (AC, AC.n)

                for ch in range(QC // TC):
                    tq0 = q * QC + ch * TC

                    @blk.scalar
                    def _(s, tq0=tq0):
                        s.wait_ge(VE.h, VE.n)
                        s.wait_ge(AC.h, AC.n)
                        for si in range(S):
                            s.activation(out=dA_c[:, si, :, :],
                                         in_=dt_r[:, :, tq0:tq0 + TC],
                                         func=AF.Exp, scale=abc_sb.ap()[:, si:si + 1]).then_inc(AC.h, 1)
                            AC.n += 1

                    @blk.gpsimd
                    def _(g, tq0=tq0, ch=ch):
                        g.wait_ge(AC.h, AC.n)
                        tqc = ch * TC
                        bb = Bb_sb[:, :, None, tqc:tqc + TC].broadcast_to([128, S, 16, TC])
                        du = dtu_r[:, None, :, tq0:tq0 + TC].broadcast_to([128, S, 16, TC])
                        g.tensor_tensor(dBu_c[:], du, bb, OP.mult).then_inc(PO.h, 1)
                        PO.n += 1
                        g.wait_ge(PO.h, PO.n)
                        g.tensor_scalar(dBu_c[:], dBu_c[:], 10.0, -10.0,
                                        OP.min, OP.max).then_inc(PO.h, 1)
                        PO.n += 1

                    @blk.vector
                    def _(v, tb=tb, tq0=tq0):
                        v.wait_ge(PO.h, PO.n)
                        for trel in range(TC):
                            tglob = tb * B + tq0 + trel
                            hprev = hring[:, (tglob - 1) % HRING, :].rearrange("p (s c) -> p s c", s=S)
                            hcur_f = hring[:, tglob % HRING, :]
                            v.scalar_tensor_tensor(v1_s[:], dA_c[:, :, :, trel], EXP_M20,
                                                   hprev, OP.max, OP.mult)
                            v.tensor_tensor(v2_s.rearrange("p (s c) -> p s c", s=S),
                                            dBu_c[:, :, :, trel], v1_s[:], OP.add)
                            v.tensor_scalar(hcur_f, v2_s, 100.0, -100.0, OP.min, OP.max)
                            tloc = tq0 + trel
                            h_cs = hring[:, tglob % HRING, :].rearrange("p (s c) -> p c s", s=S)
                            crow = pb[7].ap()[:, 0:S * QC].rearrange("p (s tq) -> p s tq", tq=QC)[:, :, tloc % QC]
                            cbt = crow[:, None, :].broadcast_to([128, 16, S])
                            v.tensor_tensor(ypre[:], h_cs, cbt, OP.mult)
                            v.reduce_sum(ybuf[:, tloc, :], ypre[:], axis=AX.X)
                        v.last.then_inc(VE.h, 1)
                    VE.n += 1
                bank_rel[7] = (VE, VE.n)

            @blk.scalar
            def _(s):
                s.wait_ge(VE.h, VE.n)
                s.activation(out=ysq, in_=ybuf, func=AF.Square)
                s.last.then_inc(AC.h, 1)
            AC.n += 1

            @blk.vector
            def _(v):
                v.wait_ge(AC.h, AC.n)
                v.reduce_sum(yds[:, 0:B], ybuf, axis=AX.X).then_inc(VE.h, 1)
                v.reduce_sum(yds[:, B:2 * B], ysq, axis=AX.X).then_inc(VE.h, 1)
                VE.n += 2

            @blk.tensor
            def _(t):
                t.wait_ge(VE.h, VE.n)
                te_take(t, 5)
                t.matmul(pb[5][0:1, 0:2 * B], onesc_sb[:], yds[:], start=True, stop=True)
                t.last.then_inc(TE.h, 1)
            TE.n += 1

            @blk.vector
            def _(v):
                v.wait_ge(TE.h, TE.n)
                mu = strow_sb.ap()[:, 0:B]
                m2 = strow_sb.ap()[:, B:2 * B]
                var = strow_sb.ap()[:, 2 * B:3 * B]
                v.tensor_scalar(mu, pb[5][0:1, 0:B], 1.0 / DI, None, OP.mult).then_inc(VE.h, 1)
                v.tensor_scalar(m2, pb[5][0:1, B:2 * B], 1.0 / DI, None, OP.mult).then_inc(VE.h, 1)
                VE.n += 2
                v.wait_ge(VE.h, VE.n)
                v.tensor_tensor(var, mu, mu, OP.mult).then_inc(VE.h, 1)
                VE.n += 1
                v.wait_ge(VE.h, VE.n)
                v.tensor_tensor(var, m2, var, OP.subtract).then_inc(VE.h, 1)
                VE.n += 1
            bank_rel[5] = (VE, VE.n)

            @blk.scalar
            def _(s):
                s.wait_ge(VE.h, VE.n)
                var = strow_sb.ap()[:, 2 * B:3 * B]
                lnv = strow_sb.ap()[:, B:2 * B]
                rstd = strow_sb.ap()[:, 3 * B:4 * B]
                s.activation(out=lnv, in_=var, func=AF.Ln, bias=1e-5).then_inc(AC.h, 1)
                AC.n += 1
                s.wait_ge(AC.h, AC.n)
                s.activation(out=rstd, in_=lnv, func=AF.Exp, scale=-0.5).then_inc(AC.h, 1)
                AC.n += 1

            @blk.tensor
            def _(t):
                t.wait_ge(AC.h, AC.n)
                te_take(t, 5)
                t.matmul(pb[5][:, 0:B], onesr_sb[:], strow_sb.ap()[:, 0:B], start=True, stop=True)
                t.matmul(pb[5][:, B:2 * B], onesr_sb[:], strow_sb.ap()[:, 3 * B:4 * B],
                         start=True, stop=True)
                t.last.then_inc(TE.h, 1)
            TE.n += 1

            @blk.vector
            def _(v):
                v.wait_ge(TE.h, TE.n)
                mub = pb[5].ap()[:, 0:B][:, :, None].broadcast_to([128, B, 16])
                rsb = pb[5].ap()[:, B:2 * B][:, :, None].broadcast_to([128, B, 16])
                v.tensor_tensor(yln[:], ybuf[:], mub, OP.subtract).then_inc(VE.h, 1)
                VE.n += 1
                v.wait_ge(VE.h, VE.n)
                v.tensor_tensor(ybuf[:], yln[:], rsb, OP.mult).then_inc(VE.h, 1)
                VE.n += 1
                v.wait_ge(VE.h, VE.n)
                for dc in range(16):
                    v.tensor_scalar(yln[:, :, dc], ybuf[:, :, dc], lng_g.ap()[:, dc:dc + 1],
                                    lnb_g.ap()[:, dc:dc + 1], OP.mult, OP.add).then_inc(VE.h, 1)
                    VE.n += 1
            bank_rel[5] = (VE, VE.n)

            @blk.vector
            def _(v):
                v.wait_ge(VE.h, VE.n)
                for dc in range(16):
                    v.scalar_tensor_tensor(xc_r[:, dc, :], u_r[:, dc, :], D_g.ap()[:, dc:dc + 1],
                                           yln[:, :, dc], OP.mult, OP.add).then_inc(VE.h, 1)
                    VE.n += 1

            @blk.gpsimd
            def _(g):
                g.wait_ge(VE.h, VE.n)
                g.tensor_tensor(y2bf[:], xc_r[:], zs_r[:], OP.mult)
                g.last.then_inc(PO.h, 1)
            PO.n += 1

            opch = [(half, kc) for half in range(2) for kc in range(16)]
            op_te_counts = {}
            for i, (half, kc) in enumerate(opch):
                @blk.sync
                def _(sync, i=i, half=half, kc=kc):
                    if i >= 6 and (i - 6) in op_te_counts:
                        sync.wait_ge(TE.h, op_te_counts[i - 6])
                    sync.dma_start(out=opw_c[i % 6],
                                   in_=opw_bf[kc * 128:(kc + 1) * 128,
                                              half * 512:(half + 1) * 512]).then_inc(LD.h, 16)
                LD.n += 16

                @blk.tensor
                def _(t, i=i, half=half, kc=kc):
                    t.wait_ge(LD.h, LD.n)
                    if i == 0:
                        t.wait_ge(PO.h, PO.n)
                        te_take(t, 0)
                        te_take(t, 1)
                    t.matmul(pb[half][:, 0:512], y2bf[:, kc, :], opw_c[i % 6],
                             start=(kc == 0), stop=(kc == 15))
                    t.last.then_inc(TE.h, 1)
                TE.n += 1
                op_te_counts[i] = TE.n

            @blk.scalar
            def _(s):
                s.wait_ge(TE.h, TE.n)
                s.wait_ge(STm.h, STm.n)
                s.activation(out=s_tile[:, 0:512], in_=pb[0][:, 0:512], func=AF.Copy).then_inc(AC.h, 1)
                s.activation(out=s_tile[:, 512:1024], in_=pb[1][:, 0:512], func=AF.Copy).then_inc(AC.h, 1)
                AC.n += 2
            bank_rel[0] = (AC, AC.n)
            bank_rel[1] = (AC, AC.n)

            @blk.sync
            def _(sync, tb=tb):
                sync.wait_ge(AC.h, AC.n)
                sync.dma_start(out=s_dram[tb * B:(tb + 1) * B, :], in_=s_tile).then_inc(STm.h, 16)
            STm.n += 16

        for tb in range(NB if DO_PH3 else 0):
            @blk.sync
            def _(sync, tb=tb):
                wait_all(sync)
                sync.dma_start(out=p3x, in_=x_in[tb * B:(tb + 1) * B, :]).then_inc(LD.h, 16)
                sync.dma_start(out=p3g, in_=g_dram[tb * B:(tb + 1) * B, :]).then_inc(LD.h, 16)
                sync.dma_start(out=p3s, in_=s_dram[tb * B:(tb + 1) * B, :]).then_inc(LD.h, 16)
            LD.n += 48

            @blk.vector
            def _(v):
                v.wait_ge(LD.h, LD.n)
                mu = small_sb.ap()[:, 0:1]
                v.reduce_sum(mu, p3g[:, None, :], axis=AX.X).then_inc(VE.h, 1)
                VE.n += 1
                v.wait_ge(VE.h, VE.n)
                v.tensor_scalar(mu, mu, 1.0 / D, None, OP.mult).then_inc(VE.h, 1)
                VE.n += 1
                v.wait_ge(VE.h, VE.n)
                v.tensor_scalar(p3t, p3g, mu, None, OP.subtract).then_inc(VE.h, 1)
                VE.n += 1

            @blk.scalar
            def _(s):
                s.wait_ge(VE.h, VE.n)
                s.activation(out=p3o, in_=p3t, func=AF.Square)
                s.last.then_inc(AC.h, 1)
            AC.n += 1

            @blk.vector
            def _(v):
                v.wait_ge(AC.h, AC.n)
                var = small_sb.ap()[:, 2:3]
                v.reduce_sum(var, p3o[:, None, :], axis=AX.X).then_inc(VE.h, 1)
                VE.n += 1
                v.wait_ge(VE.h, VE.n)
                v.tensor_scalar(var, var, 1.0 / D, None, OP.mult).then_inc(VE.h, 1)
                VE.n += 1

            @blk.scalar
            def _(s):
                s.wait_ge(VE.h, VE.n)
                var = small_sb.ap()[:, 2:3]
                lnv = small_sb.ap()[:, 4:5]
                rstd = small_sb.ap()[:, 3:4]
                s.activation(out=lnv, in_=var, func=AF.Ln, bias=1e-5).then_inc(AC.h, 1)
                AC.n += 1
                s.wait_ge(AC.h, AC.n)
                s.activation(out=rstd, in_=lnv, func=AF.Exp, scale=-0.5).then_inc(AC.h, 1)
                AC.n += 1

            @blk.vector
            def _(v):
                v.wait_ge(AC.h, AC.n)
                rstd = small_sb.ap()[:, 3:4]
                v.tensor_scalar(p3o, p3t, rstd, None, OP.mult).then_inc(VE.h, 1)
                VE.n += 1
                v.wait_ge(VE.h, VE.n)
                v.tensor_tensor(p3t, p3o, gam_sb[:], OP.mult).then_inc(VE.h, 1)
                VE.n += 1
                v.wait_ge(VE.h, VE.n)
                v.tensor_tensor(p3o, p3t, bet_sb[:], OP.add).then_inc(VE.h, 1)
                VE.n += 1

            @blk.scalar
            def _(s):
                s.wait_ge(VE.h, VE.n)
                s.activation(out=p3t, in_=p3o, func=AF.Silu)
                s.last.then_inc(AC.h, 1)
            AC.n += 1

            @blk.vector
            def _(v):
                v.wait_ge(AC.h, AC.n)
                v.tensor_tensor(p3o, p3s, p3t, OP.mult).then_inc(VE.h, 1)
                VE.n += 1
                v.wait_ge(VE.h, VE.n)
                v.tensor_tensor(p3o, p3o, p3x, OP.add).then_inc(VE.h, 1)
                VE.n += 1

            @blk.sync
            def _(sync, tb=tb):
                sync.wait_ge(VE.h, VE.n)
                sync.dma_start(out=out[tb * B:(tb + 1) * B, :], in_=p3o).then_inc(STm.h, 16)
            STm.n += 16

        @blk.sync
        def _(sync):
            sync.wait_ge(STm.h, STm.n)

    return nc


def _prep_inputs(inputs):
    f = np.float32
    ip = np.ascontiguousarray(inputs["L0_attn_in_proj_w"].T).astype(f)
    op = np.ascontiguousarray(inputs["L0_attn_out_proj_w"].T).astype(f)
    xp = inputs["L0_attn_x_proj_w"]
    xpw = np.zeros((DI, 80), f)
    xpw[:, 0] = xp[0]
    xpw[:, 32:48] = xp[1:17].T
    xpw[:, 64:80] = xp[17:33].T
    dtwr = np.ascontiguousarray(inputs["L0_attn_dt_proj_w"][:, 0][None, :]).astype(f)
    gw = np.ascontiguousarray(inputs["L0_ffn_gate_w"].T).astype(f)

    def grid(vec):
        return np.ascontiguousarray(np.asarray(vec, f).reshape(16, 128).T)

    convw = np.ascontiguousarray(
        np.asarray(inputs["L0_attn_conv_w"], f)[:, 0, :].reshape(16, 128, 4).transpose(1, 0, 2))

    sel = np.zeros((S, S * 128), f)
    for s in range(S):
        sel[s, s * 128:(s + 1) * 128] = 1.0

    cconst = np.zeros((128, 8), f)
    cconst[:, 0] = 1.0
    cconst[:, 1] = 1e-5
    cconst[:, 2] = -0.5
    cconst[:, 3] = -1.0

    common = {
        "ipwT": ip, "opwT": op, "xpw": xpw, "dtw": dtwr, "gwT": gw,
        "w1": np.ascontiguousarray(inputs["L0_ffn_expert_w1"], f),
        "w2": np.ascontiguousarray(inputs["L0_ffn_expert_w2"], f),
        "g_convb": grid(inputs["L0_attn_conv_b"]),
        "g_dtb": grid(inputs["L0_attn_dt_proj_b"]),
        "g_D": grid(inputs["L0_attn_D"]),
        "g_lng": grid(inputs["L0_attn_ln_g"]),
        "g_lnb": grid(inputs["L0_attn_ln_b"]),
        "g_convw": convw,
        "alog": np.asarray(inputs["L0_attn_A_log"], f)[None, :],
        "gamma_b": np.ascontiguousarray(np.broadcast_to(np.asarray(inputs["L0_ffn_ln_gamma"], f), (128, D))),
        "beta_b": np.ascontiguousarray(np.broadcast_to(np.asarray(inputs["L0_ffn_ln_beta"], f), (128, D))),
        "ones_col": np.ones((128, 1), f),
        "ones_row": np.ones((1, 128), f),
        "sel16": sel,
        "cconst": cconst,
        "ident_in": np.eye(128, dtype=f),
    }
    x = np.asarray(inputs["L0_attn_x"], f)
    in_maps = []
    for b in range(NCORE):
        m = dict(common)
        m["x"] = np.ascontiguousarray(x[b])
        in_maps.append(m)
    return in_maps


def kernel(**inputs):
    in_maps = _prep_inputs(inputs)
    nc = build(debug=False)
    res = run_bass_kernel_spmd(nc, in_maps, core_ids=list(range(NCORE)))
    outs = [res.results[b]["out"] for b in range(NCORE)]
    return np.stack(outs, axis=0).astype(np.float32)


# version 05 (final)
# speedup vs baseline: 1.8426x; 1.2259x over previous
import contextlib
import numpy as np
import ml_dtypes
import concourse.bass as bass
import concourse.mybir as mybir
from concourse.bass_utils import run_bass_kernel_spmd

F32 = mybir.dt.float32
BF16 = mybir.dt.bfloat16
U32 = mybir.dt.uint32
U8 = mybir.dt.uint8
AF = mybir.ActivationFunctionType
OP = mybir.AluOpType
AX = mybir.AxisListType

BATCH, T, D, DI, S, E, H, NCORE = 8, 4096, 1024, 2048, 16, 8, 2048, 8
B = 128
NB = T // B
TC = 8
QC = 32
HRING = 2
TG = 512
NTG = T // TG
NTC = TG // 128
EXP_M20 = float(np.exp(np.float32(-20.0)))
DO_PH1 = True
DO_PH2 = True
DO_PH3 = True


class Sem:
    def __init__(self, es, nc, name):
        self.h = es.enter_context(nc.semaphore(name))
        self.n = 0


_NOTRACK = {"wait_ge", "nop", "engine_nop"}


class EngLast:

    def __init__(self, eng):
        object.__setattr__(self, "_e", eng)
        object.__setattr__(self, "last", None)

    def __getattr__(self, name):
        f = getattr(self._e, name)
        if callable(f) and name not in _NOTRACK and not name.startswith("_"):
            def wrap(*a, **k):
                r = f(*a, **k)
                if r is not None and hasattr(r, "then_inc"):
                    object.__setattr__(self, "last", r)
                return r
            return wrap
        return f


class WBlk:
    def __init__(self, blk):
        self._b = blk

    def _mk(self, name):
        dec = getattr(self._b, name)

        def deco(fn):
            def inner(eng):
                return fn(EngLast(eng))
            return dec(inner)
        return deco

    @property
    def vector(self):
        return self._mk("vector")

    @property
    def scalar(self):
        return self._mk("scalar")

    @property
    def tensor(self):
        return self._mk("tensor")

    @property
    def gpsimd(self):
        return self._mk("gpsimd")

    @property
    def sync(self):
        return self._mk("sync")


def build(debug=False):
    nc = bass.Bass("TRN2", target_bir_lowering=False)

    x_in = nc.dram_tensor("x", [T, D], F32, kind="ExternalInput").ap()
    ipwT = nc.dram_tensor("ipwT", [D, 2 * DI], F32, kind="ExternalInput").ap()
    opwT = nc.dram_tensor("opwT", [DI, D], F32, kind="ExternalInput").ap()
    xpw = nc.dram_tensor("xpw", [DI, 80], F32, kind="ExternalInput").ap()
    dtw = nc.dram_tensor("dtw", [1, DI], F32, kind="ExternalInput").ap()
    gwT = nc.dram_tensor("gwT", [D, E], F32, kind="ExternalInput").ap()
    w1 = nc.dram_tensor("w1", [E, D, H], F32, kind="ExternalInput").ap()
    w2 = nc.dram_tensor("w2", [E, H, D], F32, kind="ExternalInput").ap()
    g_convb = nc.dram_tensor("g_convb", [128, 16], F32, kind="ExternalInput").ap()
    g_dtb = nc.dram_tensor("g_dtb", [128, 16], F32, kind="ExternalInput").ap()
    g_D = nc.dram_tensor("g_D", [128, 16], F32, kind="ExternalInput").ap()
    g_lng = nc.dram_tensor("g_lng", [128, 16], F32, kind="ExternalInput").ap()
    g_lnb = nc.dram_tensor("g_lnb", [128, 16], F32, kind="ExternalInput").ap()
    g_convw = nc.dram_tensor("g_convw", [128, 16, 4], F32, kind="ExternalInput").ap()
    alog = nc.dram_tensor("alog", [1, S], F32, kind="ExternalInput").ap()
    gamma_b = nc.dram_tensor("gamma_b", [128, D], F32, kind="ExternalInput").ap()
    beta_b = nc.dram_tensor("beta_b", [128, D], F32, kind="ExternalInput").ap()
    ones_col = nc.dram_tensor("ones_col", [128, 1], F32, kind="ExternalInput").ap()
    ones_row = nc.dram_tensor("ones_row", [1, 128], F32, kind="ExternalInput").ap()
    sel16 = nc.dram_tensor("sel16", [S, S * 128], F32, kind="ExternalInput").ap()
    cconst = nc.dram_tensor("cconst", [128, 8], F32, kind="ExternalInput").ap()
    ident_in = nc.dram_tensor("ident_in", [128, 128], F32, kind="ExternalInput").ap()

    out = nc.dram_tensor("out", [T, D], F32, kind="ExternalOutput").ap()
    okind = "ExternalOutput" if debug else "Internal"
    xn_dram = nc.dram_tensor("xn_dram", [T, D], BF16, kind="Internal").ap()
    g_dram = nc.dram_tensor("g_dram", [T, D], F32, kind=okind).ap()
    s_dram = nc.dram_tensor("s_dram", [T, D], F32, kind=okind).ap()
    xn_f32 = nc.dram_tensor("xn_f32", [T, D], F32, kind=okind).ap() if debug else None
    opw_bf = nc.dram_tensor("opw_bf", [DI, D], BF16, kind="Internal").ap()

    es = contextlib.ExitStack()
    with es:
        ipw_sb = es.enter_context(nc.sbuf_tensor("ipw_sb", [128, 8, 2 * DI], BF16))
        convb_g = es.enter_context(nc.sbuf_tensor("convb_g", [128, 16], F32))
        dtb_g = es.enter_context(nc.sbuf_tensor("dtb_g", [128, 16], F32))
        D_g = es.enter_context(nc.sbuf_tensor("D_g", [128, 16], F32))
        lng_g = es.enter_context(nc.sbuf_tensor("lng_g", [128, 16], F32))
        lnb_g = es.enter_context(nc.sbuf_tensor("lnb_g", [128, 16], F32))
        convw_g = es.enter_context(nc.sbuf_tensor("convw_g", [128, 16, 4], F32))
        alog_sb = es.enter_context(nc.sbuf_tensor("alog_sb", [1, S], F32))
        arow_sb = es.enter_context(nc.sbuf_tensor("arow_sb", [1, S], F32))
        abc_sb = es.enter_context(nc.sbuf_tensor("abc_sb", [128, S], F32))
        gam_sb = es.enter_context(nc.sbuf_tensor("gam_sb", [128, D], F32))
        bet_sb = es.enter_context(nc.sbuf_tensor("bet_sb", [128, D], F32))
        onesc_sb = es.enter_context(nc.sbuf_tensor("onesc_sb", [128, 1], F32))
        onesr_sb = es.enter_context(nc.sbuf_tensor("onesr_sb", [1, 128], F32))
        sel_sb = es.enter_context(nc.sbuf_tensor("sel_sb", [S, S * 128], F32))
        cc_sb = es.enter_context(nc.sbuf_tensor("cc_sb", [128, 8], F32))
        xpw_sb = es.enter_context(nc.sbuf_tensor("xpw_sb", [128, 16, 80], F32))
        dtw_sb = es.enter_context(nc.sbuf_tensor("dtw_sb", [1, DI], F32))
        gw_sb = es.enter_context(nc.sbuf_tensor("gw_sb", [128, 8, E], F32))
        ident_sb = es.enter_context(nc.sbuf_tensor("ident_sb", [128, 128], F32))
        mtv_sb = es.enter_context(nc.sbuf_tensor("mtv_sb", [128, NB, E], F32))
        dtraw_sb = es.enter_context(nc.sbuf_tensor("dtraw_sb", [1, B], F32))
        strow_sb = es.enter_context(nc.sbuf_tensor("strow_sb", [1, 4 * B], F32))
        BC_sb = es.enter_context(nc.sbuf_tensor("BC_sb", [16, 2 * B], F32))
        small_sb = es.enter_context(nc.sbuf_tensor("small_sb", [128, 32], F32))
        smallu_sb = es.enter_context(nc.sbuf_tensor("smallu_sb", [128, 8], U32))

        ARENA_BYTES = 108 * 1024
        arena = es.enter_context(nc.sbuf_tensor("arena", [128, ARENA_BYTES], U8))
        aap = arena.ap()

        class Carver:
            def __init__(self):
                self.off = 0

            def take(self, shape, dtype, name=""):
                n = int(np.prod(shape[1:]))
                nbytes = n * mybir.dt.size(dtype)
                off = (self.off + 31) // 32 * 32
                assert off + nbytes <= ARENA_BYTES, f"arena overflow at {name}: {off}+{nbytes}"
                self.off = off + nbytes
                ap = aap[:, off:off + nbytes].bitcast(dtype)
                if len(shape) > 2:
                    pat = "p (" + " ".join(f"d{i}" for i in range(1, len(shape))) + ") -> p " + \
                          " ".join(f"d{i}" for i in range(1, len(shape)))
                    kw = {f"d{i}": shape[i] for i in range(1, len(shape) - 1)}
                    ap = ap.rearrange(pat, **kw)
                return ap

        c = Carver()
        _xt = c.take([128, D], F32, "xt0")
        xtile = [_xt, _xt]
        psq = c.take([128, D], F32, "psq")
        xnf32 = c.take([128, D], F32, "xnf32")
        xnT = psq.rearrange("p (a b) -> p a b", a=8)
        xn_out = c.take([128, D], BF16, "xn_out")
        gate_e = c.take([128, E], F32, "gate_e")
        _xnt = c.take([128, 8, TG], BF16, "xnt0")
        xnt = [_xnt, _xnt]
        w1_sb = c.take([128, 8, H], BF16, "w1_sb")
        w2_sb = c.take([128, 16, D], BF16, "w2_sb")
        heg = c.take([128, 16, TG], BF16, "heg")
        gtile = c.take([128, D], F32, "gtile")
        gate_sb = c.take([128, E], F32, "gate_sb")
        ph01_high = c.off

        c = Carver()
        xnp = c.take([128, 8, B], BF16, "xnp")
        x_in0 = c.take([128, 16, B + 3], F32, "x_in0")
        xc_r = c.take([128, 16, B], F32, "xc_r")
        u_r = c.take([128, 16, B], F32, "u_r")
        zs_r = c.take([128, 16, B], F32, "zs_r")
        dt_r = c.take([128, 16, B], F32, "dt_r")
        dtu_r = c.take([128, 16, B], F32, "dtu_r")
        dA_c = c.take([128, S, 16, TC], F32, "dA_c")
        dBu_c = c.take([128, S, 16, TC], F32, "dBu_c")
        hring = c.take([128, HRING, 256], F32, "hring")
        v1_s = c.take([128, S, 16], F32, "v1")
        v2_s = c.take([128, 256], F32, "v2")
        ypre = c.take([128, 16, S], F32, "ypre")
        ybuf = c.take([128, B, 16], F32, "ybuf")
        ysq = c.take([128, B, 16], F32, "ysq")
        yds = c.take([128, 2 * B], F32, "yds")
        yln = ysq
        Bb_sb = c.take([128, S, QC], F32, "Bb_sb")
        y2bf = c.take([128, 16, B], BF16, "y2bf")
        opw_c = [c.take([128, 512], BF16, f"opw{i}") for i in range(6)]
        s_tile = c.take([128, D], F32, "s_tile")
        spt = [c.take([128, B], F32, f"spt{i}") for i in range(3)]
        ph2_high = c.off

        dA_bufs = [dA_c,
                   xc_r.rearrange("p a b -> p (a b)").rearrange("p (s c t) -> p s c t", s=S, c=16)]
        dBu_bufs = [dBu_c,
                    ysq.rearrange("p a b -> p (a b)").rearrange("p (s c t) -> p s c t", s=S, c=16)]
        Bb_bufs = [Bb_sb,
                   s_tile[:, 0:S * QC].rearrange("p (a b) -> p a b", a=S)]

        c = Carver()
        p3x = c.take([128, D], F32, "p3x")
        p3g = c.take([128, D], F32, "p3g")
        p3s = c.take([128, D], F32, "p3s")
        p3t = c.take([128, D], F32, "p3t")
        p3o = c.take([128, D], F32, "p3o")
        ph3_high = c.off

        if debug:
            print(f"[build] arena usage: ph01={ph01_high} ph2={ph2_high} ph3={ph3_high} of {ARENA_BYTES}")
            print(f"[build] sbuf remaining: {nc.sbuf_bytes_remaining}")

        pb = [es.enter_context(nc.psum_tensor(f"pb{i}", [128, 512], F32)) for i in range(8)]

        LD = Sem(es, nc, "LD")
        LDg = Sem(es, nc, "LDg")
        STm = Sem(es, nc, "STm")
        STg = Sem(es, nc, "STg")
        TE = Sem(es, nc, "TE")
        AC = Sem(es, nc, "AC")
        VE = Sem(es, nc, "VE")
        PO = Sem(es, nc, "PO")
        allsems = [LD, LDg, STm, STg, TE, AC, VE, PO]

        blk = WBlk(es.enter_context(nc.Block()))

        def wait_all(eng):
            for sm in allsems:
                if sm.n > 0:
                    eng.wait_ge(sm.h, sm.n)

        bank_rel = [None] * 8

        def te_take(t, i):
            if bank_rel[i] is not None:
                sm, n = bank_rel[i]
                t.wait_ge(sm.h, n)
                bank_rel[i] = None

        @blk.sync
        def _(sync):
            for dst, src in [(convb_g, g_convb), (dtb_g, g_dtb), (D_g, g_D), (lng_g, g_lng),
                             (lnb_g, g_lnb), (alog_sb, alog), (gam_sb, gamma_b), (bet_sb, beta_b),
                             (onesc_sb, ones_col), (onesr_sb, ones_row), (sel_sb, sel16),
                             (cc_sb, cconst), (dtw_sb, dtw)]:
                sync.dma_start(out=dst[:], in_=src).then_inc(LD.h, 16)
                LD.n += 16
            sync.dma_start(out=convw_g[:], in_=g_convw).then_inc(LD.h, 16)
            LD.n += 16
            sync.dma_start(out=xpw_sb[:], in_=xpw.rearrange("(c p) m -> p c m", p=128)).then_inc(LD.h, 16)
            LD.n += 16

        @blk.gpsimd
        def _(g):
            g.dma_start(out=gw_sb[:], in_=gwT.rearrange("(c p) m -> p c m", p=128)).then_inc(LDg.h, 16)
            LDg.n += 16
            g.dma_start(out=ident_sb[:], in_=ident_in).then_inc(LDg.h, 16)
            LDg.n += 16
            g.dma_start(out=opw_bf, in_=opwT).then_inc(LDg.h, 16)
            LDg.n += 16

        nc.const_aps.register(F32, 1.0, cc_sb.ap()[:, 0:1])
        nc.const_aps.register(F32, 1e-5, cc_sb.ap()[:, 1:2])
        nc.const_aps.register(F32, -0.5, cc_sb.ap()[:, 2:3])
        nc.const_aps.register(F32, -1.0, cc_sb.ap()[:, 3:4])

        @blk.vector
        def _(v):
            wait_all(v)
            v.tensor_scalar(arow_sb[:], alog_sb[:], 5.0, None, OP.min)
            v.last.then_inc(VE.h, 1)
        VE.n += 1

        @blk.scalar
        def _(s):
            s.wait_ge(VE.h, VE.n)
            s.activation(out=alog_sb[:], in_=arow_sb[:], func=AF.Exp)
            s.last.then_inc(AC.h, 1)
        AC.n += 1

        @blk.tensor
        def _(t):
            t.wait_ge(AC.h, AC.n)
            t.matmul(pb[5][:, 0:S], onesr_sb[:], alog_sb[:], start=True, stop=True)
            t.last.then_inc(TE.h, 1)
        TE.n += 1

        @blk.vector
        def _(v):
            v.wait_ge(TE.h, TE.n)
            v.tensor_scalar(abc_sb[:], pb[5][:, 0:S], -1.0, None, OP.mult)
            v.last.then_inc(VE.h, 1)
        VE.n += 1
        bank_rel[5] = (VE, VE.n)

        for tb in range(NB):
            buf = xtile[tb % 2]

            @blk.sync
            def _(sync, tb=tb, buf=buf):
                wait_all(sync)
                sync.dma_start(out=buf, in_=x_in[tb * B:(tb + 1) * B, :]).then_inc(LD.h, 16)
            LD.n += 16

            @blk.scalar
            def _(s, buf=buf):
                s.wait_ge(LD.h, LD.n)
                s.activation(out=psq, in_=buf, func=AF.Square)
                s.last.then_inc(AC.h, 1)
            AC.n += 1

            @blk.vector
            def _(v, buf=buf):
                v.wait_ge(AC.h, AC.n)
                mu = small_sb.ap()[:, 0:1]
                m2 = small_sb.ap()[:, 1:2]
                nvar = small_sb.ap()[:, 2:3]
                v.reduce_sum(mu, buf[:, None, :], axis=AX.X).then_inc(VE.h, 1)
                v.reduce_sum(m2, psq[:, None, :], axis=AX.X).then_inc(VE.h, 1)
                VE.n += 2
                v.wait_ge(VE.h, VE.n)
                v.tensor_scalar(mu, mu, 1.0 / D, None, OP.mult).then_inc(VE.h, 1)
                v.tensor_scalar(m2, m2, 1.0 / D, None, OP.mult).then_inc(VE.h, 1)
                VE.n += 2
                v.wait_ge(VE.h, VE.n)
                v.scalar_tensor_tensor(nvar, mu, mu, m2, OP.mult, OP.subtract).then_inc(VE.h, 1)
                VE.n += 1

            @blk.scalar
            def _(s):
                s.wait_ge(VE.h, VE.n)
                nvar = small_sb.ap()[:, 2:3]
                lnv = small_sb.ap()[:, 17:18]
                rstd = small_sb.ap()[:, 3:4]
                s.activation(out=lnv, in_=nvar, func=AF.Ln, scale=-1.0, bias=1e-5).then_inc(AC.h, 1)
                AC.n += 1
                s.wait_ge(AC.h, AC.n)
                s.activation(out=rstd, in_=lnv, func=AF.Exp, scale=-0.5).then_inc(AC.h, 1)
                AC.n += 1

            @blk.vector
            def _(v, buf=buf):
                v.wait_ge(AC.h, AC.n)
                v.wait_ge(STm.h, STm.n)
                v.tensor_scalar(xnf32, buf, small_sb.ap()[:, 0:1], small_sb.ap()[:, 3:4],
                                OP.subtract, OP.mult).then_inc(VE.h, 1)
                VE.n += 1
                v.wait_ge(VE.h, VE.n)
                v.tensor_copy(xn_out, xnf32).then_inc(VE.h, 1)
                VE.n += 1

            @blk.sync
            def _(sync, tb=tb):
                sync.wait_ge(VE.h, VE.n)
                sync.dma_start(out=xn_dram[tb * B:(tb + 1) * B, :], in_=xn_out).then_inc(STm.h, 16)
            STm.n += 16

            for dc in range(8):
                bk = dc % 2

                @blk.tensor
                def _(t, dc=dc, bk=bk):
                    if dc == 0:
                        t.wait_ge(VE.h, VE.n)
                        t.wait_ge(LDg.h, LDg.n)
                    te_take(t, bk)
                    t.transpose(pb[bk][:, 0:128], xnf32[:, dc * 128:(dc + 1) * 128], ident_sb[:])
                    t.last.then_inc(TE.h, 1)
                TE.n += 1

                @blk.scalar
                def _(s, dc=dc, bk=bk):
                    s.wait_ge(TE.h, TE.n)
                    s.activation(out=xnT[:, dc, :], in_=pb[bk][:, 0:128], func=AF.Copy)
                    s.last.then_inc(AC.h, 1)
                AC.n += 1
                bank_rel[bk] = (AC, AC.n)

            @blk.tensor
            def _(t):
                t.wait_ge(AC.h, AC.n)
                te_take(t, 5)
                for kc in range(8):
                    t.matmul(pb[5][:, 0:E], xnT[:, kc, :], gw_sb.ap()[:, kc, :],
                             start=(kc == 0), stop=(kc == 7))
                t.last.then_inc(TE.h, 1)
            TE.n += 1

            @blk.vector
            def _(v):
                v.wait_ge(TE.h, TE.n)
                mx0 = small_sb.ap()[:, 4:5]
                v.tensor_copy(gate_sb, pb[5][:, 0:E]).then_inc(VE.h, 1)
                VE.n += 1
                v.wait_ge(VE.h, VE.n)
                v.reduce_max(mx0, gate_sb[:, None, :], axis=AX.X).then_inc(VE.h, 1)
                VE.n += 1
                v.wait_ge(VE.h, VE.n)
                v.tensor_scalar(small_sb.ap()[:, 12:13], mx0, -1.0, None, OP.mult).then_inc(VE.h, 1)
                VE.n += 1
            bank_rel[5] = (VE, VE.n)

            @blk.scalar
            def _(s):
                s.wait_ge(VE.h, VE.n)
                s.activation(out=gate_e, in_=gate_sb, func=AF.Exp, bias=small_sb.ap()[:, 12:13])
                s.last.then_inc(AC.h, 1)
            AC.n += 1

            @blk.vector
            def _(v, tb=tb):
                v.wait_ge(AC.h, AC.n)
                ssum = small_sb.ap()[:, 13:14]
                tv = small_sb.ap()[:, 14:15]
                mx0 = small_sb.ap()[:, 4:5]
                v.reduce_sum(ssum, gate_e[:, None, :], axis=AX.X).then_inc(VE.h, 1)
                VE.n += 1
                v.wait_ge(VE.h, VE.n)
                v.tensor_scalar(tv, ssum, 1e-8, 1.0, OP.mult, OP.add).then_inc(VE.h, 1)
                VE.n += 1
                v.wait_ge(VE.h, VE.n)
                v.reciprocal(tv, tv).then_inc(VE.h, 1)
                VE.n += 1
                v.wait_ge(VE.h, VE.n)
                for e in range(E):
                    v.scalar_tensor_tensor(mtv_sb.ap()[:, tb, e:e + 1], gate_sb[:, e:e + 1],
                                           mx0, tv, OP.is_equal, OP.mult).then_inc(VE.h, 1)
                    VE.n += 1

        if debug:
            @blk.gpsimd
            def _(g):
                wait_all(g)
                g.dma_start(out=xn_f32, in_=xn_dram).then_inc(STg.h, 16)
            STg.n += 16

        _ph1 = DO_PH1
        def load_xnt(tg, buf):
            @blk.sync
            def _(sync, tg=tg, buf=buf):
                wait_all(sync)
                for dc in range(8):
                    sync.dma_start(
                        out=buf[:, dc, :],
                        in_=xn_dram[tg * TG:(tg + 1) * TG, dc * 128:(dc + 1) * 128],
                        transpose=True,
                    ).then_inc(LD.h, 16)
            LD.n += 8 * 16

        for e in range(E if DO_PH1 else 0):
            @blk.gpsimd
            def _(g, e=e):
                wait_all(g)
                g.dma_start(out=w1_sb, in_=w1[e].rearrange("(c p) h -> p c h", p=128)).then_inc(LDg.h, 16)
                g.dma_start(out=w2_sb, in_=w2[e].rearrange("(c p) h -> p c h", p=128)).then_inc(LDg.h, 16)
            LDg.n += 32

            for tg in range(NTG):
                buf = xnt[tg % 2]
                load_xnt(tg, buf)

                for hc in range(16):
                    bk = hc % 2

                    @blk.tensor
                    def _(t, buf=buf, hc=hc, bk=bk):
                        if hc == 0:
                            t.wait_ge(LD.h, LD.n)
                            t.wait_ge(LDg.h, LDg.n)
                        te_take(t, bk)
                        for kc in range(8):
                            t.matmul(pb[bk][:, 0:TG], w1_sb[:, kc, hc * 128:(hc + 1) * 128],
                                     buf[:, kc, 0:TG], start=(kc == 0), stop=(kc == 7))
                        t.last.then_inc(TE.h, 1)
                    TE.n += 1

                    @blk.scalar
                    def _(s, hc=hc, bk=bk):
                        s.wait_ge(TE.h, TE.n)
                        s.activation(out=heg[:, hc, :], in_=pb[bk][:, 0:TG], func=AF.Gelu)
                        s.last.then_inc(AC.h, 1)
                    AC.n += 1
                    bank_rel[bk] = (AC, AC.n)

                for tc in range(NTC):
                    tbi = tg * NTC + tc

                    @blk.tensor
                    def _(t, tc=tc):
                        t.wait_ge(AC.h, AC.n)
                        te_take(t, 2)
                        te_take(t, 3)
                        for half in range(2):
                            for kc in range(16):
                                t.matmul(pb[2 + half][:, 0:512],
                                         heg[:, kc, tc * 128:(tc + 1) * 128],
                                         w2_sb[:, kc, half * 512:(half + 1) * 512],
                                         start=(kc == 0), stop=(kc == 15))
                        t.last.then_inc(TE.h, 1)
                    TE.n += 1

                    @blk.vector
                    def _(v, e=e, tbi=tbi):
                        v.wait_ge(TE.h, TE.n)
                        v.wait_ge(STg.h, STg.n)
                        mtv = mtv_sb.ap()[:, tbi, e:e + 1]
                        v.tensor_scalar(gtile[:, 0:512], pb[2][:, 0:512], mtv, None,
                                        OP.mult).then_inc(VE.h, 1)
                        v.tensor_scalar(gtile[:, 512:1024], pb[3][:, 0:512], mtv, None,
                                        OP.mult).then_inc(VE.h, 1)
                        VE.n += 2
                    bank_rel[2] = (VE, VE.n)
                    bank_rel[3] = (VE, VE.n)

                    @blk.gpsimd
                    def _(g, e=e, tbi=tbi):
                        g.wait_ge(VE.h, VE.n)
                        if e == 0:
                            g.dma_start(out=g_dram[tbi * B:(tbi + 1) * B, :], in_=gtile).then_inc(STg.h, 16)
                        else:
                            g.dma_start(out=g_dram[tbi * B:(tbi + 1) * B, :], in_=gtile,
                                        accum_op=OP.add).then_inc(STg.h, 16)
                    STg.n += 16

        @blk.gpsimd
        def _(g):
            wait_all(g)
            g.dma_start(out=ipw_sb[:], in_=ipwT.rearrange("(c p) m -> p c m", p=128)).then_inc(LDg.h, 16)
        LDg.n += 16

        @blk.vector
        def _(v):
            wait_all(v)
            v.memset(hring[:, HRING - 1, :], 0.0)
            v.memset(x_in0[:, :, 0:3], 0.0)
            v.last.then_inc(VE.h, 1)
        VE.n += 1

        for tb in range(NB if DO_PH2 else 0):
            t0g = tb * B

            @blk.sync
            def _(sync, t0g=t0g):
                wait_all(sync)
                for dc in range(8):
                    sync.dma_start(
                        out=xnp[:, dc, :],
                        in_=xn_dram[t0g:t0g + B, dc * 128:(dc + 1) * 128],
                        transpose=True,
                    ).then_inc(LD.h, 16)
            LD.n += 8 * 16

            for mc in range(32):
                bk = mc % 4

                @blk.tensor
                def _(t, mc=mc, bk=bk):
                    if mc == 0:
                        wait_all(t)
                    te_take(t, bk)
                    for kc in range(8):
                        t.matmul(pb[bk][:, 0:B], ipw_sb[:, kc, mc * 128:(mc + 1) * 128],
                                 xnp[:, kc, :], start=(kc == 0), stop=(kc == 7))
                    t.last.then_inc(TE.h, 1)
                TE.n += 1

                @blk.scalar
                def _(s, mc=mc, bk=bk):
                    s.wait_ge(TE.h, TE.n)
                    if mc < 16:
                        s.activation(out=x_in0[:, mc, 3:3 + B], in_=pb[bk][:, 0:B], func=AF.Copy)
                    else:
                        s.activation(out=zs_r[:, mc - 16, :], in_=pb[bk][:, 0:B], func=AF.Silu)
                    s.last.then_inc(AC.h, 1)
                AC.n += 1
                bank_rel[bk] = (AC, AC.n)

            @blk.vector
            def _(v):
                v.wait_ge(AC.h, AC.n)
                for dc in range(16):
                    wcol = convw_g.ap()[:, dc, :]
                    v.tensor_scalar(xc_r[:, dc, :], x_in0[:, dc, 0:B], wcol[:, 0:1],
                                    convb_g.ap()[:, dc:dc + 1], OP.mult, OP.add)
                for k in range(1, 4):
                    for dc in range(16):
                        wcol = convw_g.ap()[:, dc, :]
                        r = v.scalar_tensor_tensor(xc_r[:, dc, :], x_in0[:, dc, k:k + B],
                                                   wcol[:, k:k + 1], xc_r[:, dc, :], OP.mult, OP.add)
                        if k == 3:
                            r.then_inc(VE.h, 1)
                            VE.n += 1
                v.tensor_copy(x_in0[:, :, 0:3], x_in0[:, :, B:B + 3]).then_inc(VE.h, 1)
                VE.n += 1

            @blk.scalar
            def _(s):
                s.wait_ge(VE.h, VE.n)
                s.activation(out=u_r[:], in_=xc_r[:], func=AF.Silu)
                s.last.then_inc(AC.h, 1)
            AC.n += 1

            @blk.tensor
            def _(t):
                t.wait_ge(AC.h, AC.n)
                te_take(t, 4)
                for kc in range(16):
                    t.matmul(pb[4][0:80, 0:B], xpw_sb[:, kc, :], u_r[:, kc, :],
                             start=(kc == 0), stop=(kc == 15))
                t.last.then_inc(TE.h, 1)
            TE.n += 1

            @blk.vector
            def _(v):
                v.wait_ge(TE.h, TE.n)
                v.tensor_copy(dtraw_sb[:], pb[4][0:1, 0:B]).then_inc(VE.h, 1)
                v.tensor_copy(BC_sb.ap()[:, 0:B], pb[4][32:48, 0:B]).then_inc(VE.h, 1)
                v.tensor_copy(BC_sb.ap()[:, B:2 * B], pb[4][64:80, 0:B]).then_inc(VE.h, 1)
                VE.n += 3
            bank_rel[4] = (VE, VE.n)

            for bi in range(4):
                @blk.tensor
                def _(t, bi=bi):
                    if bi == 0:
                        t.wait_ge(VE.h, VE.n)
                    te_take(t, 4 + bi)
                    for j in range(4):
                        dc = bi * 4 + j
                        t.matmul(pb[4 + bi][:, j * B:(j + 1) * B],
                                 dtw_sb.ap()[:, dc * 128:(dc + 1) * 128],
                                 dtraw_sb[:], start=True, stop=True)
                    t.last.then_inc(TE.h, 1)
                TE.n += 1

            @blk.scalar
            def _(s):
                s.wait_ge(TE.h, TE.n)
                s.wait_ge(PO.h, PO.n)
                for dc in range(16):
                    bi, j = dc // 4, dc % 4
                    s.activation(out=dtu_r[:, dc, :], in_=pb[4 + bi][:, j * B:(j + 1) * B],
                                 func=AF.Exp, bias=dtb_g.ap()[:, dc:dc + 1]).then_inc(AC.h, 1)
                    AC.n += 1
                s.wait_ge(AC.h, AC.n)
                for dc in range(16):
                    s.activation(out=dt_r[:, dc, :], in_=dtu_r[:, dc, :],
                                 func=AF.Ln, bias=1.0).then_inc(AC.h, 1)
                    AC.n += 1
            for bi in range(4):
                bank_rel[4 + bi] = (AC, AC.n)

            @blk.vector
            def _(v):
                v.wait_ge(AC.h, AC.n)
                v.tensor_scalar(dt_r[:], dt_r[:], 10.0, 1e-4,
                                OP.min, OP.max).then_inc(VE.h, 1)
                VE.n += 1
            ve_dt = VE.n

            @blk.gpsimd
            def _(g):
                g.wait_ge(VE.h, VE.n)
                g.wait_ge(AC.h, AC.n)
                g.tensor_tensor(dtu_r[:], dt_r[:], u_r[:], OP.mult)
                g.last.then_inc(PO.h, 1)
            PO.n += 1
            po_dtu = PO.n
            ve_scan = {}
            po_after_q = {}

            for q in range(4):
                cb = 7 if q % 2 == 0 else 3
                bbuf = Bb_bufs[q % 2]

                @blk.tensor
                def _(t, q=q, cb=cb):
                    if q == 0:
                        t.wait_ge(PO.h, PO.n)
                    te_take(t, 6)
                    te_take(t, cb)
                    for si in range(S):
                        t.matmul(pb[6][:, si * QC:(si + 1) * QC],
                                 sel_sb.ap()[:, si * 128:(si + 1) * 128],
                                 BC_sb.ap()[:, q * QC:(q + 1) * QC], start=True, stop=True)
                        t.matmul(pb[cb][:, si * QC:(si + 1) * QC],
                                 sel_sb.ap()[:, si * 128:(si + 1) * 128],
                                 BC_sb.ap()[:, B + q * QC:B + (q + 1) * QC], start=True, stop=True)
                    t.last.then_inc(TE.h, 1)
                TE.n += 1
                te_q = TE.n

                @blk.scalar
                def _(s, q=q, bbuf=bbuf):
                    s.wait_ge(TE.h, TE.n)
                    if q >= 2:
                        s.wait_ge(PO.h, po_after_q[q - 2])
                    s.wait_ge(STm.h, STm.n)
                    s.activation(out=bbuf.rearrange("p a b -> p (a b)"),
                                 in_=pb[6].ap()[:, 0:S * QC], func=AF.Copy)
                    s.last.then_inc(AC.h, 1)
                AC.n += 1
                bank_rel[6] = (AC, AC.n)
                ac_bb = AC.n

                for ch in range(QC // TC):
                    ci = q * (QC // TC) + ch
                    tq0 = q * QC + ch * TC
                    dAb = dA_bufs[ci % 2]
                    dBb = dBu_bufs[ci % 2]

                    @blk.scalar
                    def _(s, tq0=tq0, ci=ci, dAb=dAb):
                        s.wait_ge(VE.h, ve_scan.get(ci - 2, ve_dt))
                        for si in range(S):
                            s.activation(out=dAb[:, si, :, :],
                                         in_=dt_r[:, :, tq0:tq0 + TC],
                                         func=AF.Exp, scale=abc_sb.ap()[:, si:si + 1]).then_inc(AC.h, 1)
                            AC.n += 1
                    ac_da = AC.n

                    @blk.gpsimd
                    def _(g, tq0=tq0, ch=ch, ci=ci, dBb=dBb, bbuf=bbuf, ac_bb=ac_bb):
                        g.wait_ge(AC.h, ac_bb)
                        g.wait_ge(PO.h, po_dtu)
                        w = ve_scan.get(ci - 2, 0)
                        if w:
                            g.wait_ge(VE.h, w)
                        tqc = ch * TC
                        bb = bbuf[:, :, None, tqc:tqc + TC].broadcast_to([128, S, 16, TC])
                        du = dtu_r[:, None, :, tq0:tq0 + TC].broadcast_to([128, S, 16, TC])
                        g.tensor_tensor(dBb[:], du, bb, OP.mult).then_inc(PO.h, 1)
                        PO.n += 1
                        g.wait_ge(PO.h, PO.n)
                        g.tensor_scalar(dBb[:], dBb[:], 10.0, -10.0,
                                        OP.min, OP.max).then_inc(PO.h, 1)
                        PO.n += 1
                    po_dbu = PO.n

                    @blk.vector
                    def _(v, tb=tb, tq0=tq0, te_q=te_q, ac_da=ac_da, po_dbu=po_dbu,
                          dAb=dAb, dBb=dBb, cb=cb):
                        v.wait_ge(TE.h, te_q)
                        v.wait_ge(AC.h, ac_da)
                        v.wait_ge(PO.h, po_dbu)
                        for trel in range(TC):
                            tglob = tb * B + tq0 + trel
                            hprev = hring[:, (tglob - 1) % HRING, :].rearrange("p (s c) -> p s c", s=S)
                            hcur_f = hring[:, tglob % HRING, :]
                            v.scalar_tensor_tensor(v1_s[:], dAb[:, :, :, trel], EXP_M20,
                                                   hprev, OP.max, OP.mult)
                            v.tensor_tensor(v2_s.rearrange("p (s c) -> p s c", s=S),
                                            dBb[:, :, :, trel], v1_s[:], OP.add)
                            v.tensor_scalar(hcur_f, v2_s, 100.0, -100.0, OP.min, OP.max)
                            tloc = tq0 + trel
                            h_cs = hring[:, tglob % HRING, :].rearrange("p (s c) -> p c s", s=S)
                            crow = pb[cb].ap()[:, 0:S * QC].rearrange("p (s tq) -> p s tq", tq=QC)[:, :, tloc % QC]
                            cbt = crow[:, None, :].broadcast_to([128, 16, S])
                            v.tensor_tensor(ypre[:], h_cs, cbt, OP.mult)
                            v.reduce_sum(ybuf[:, tloc, :], ypre[:], axis=AX.X)
                        v.last.then_inc(VE.h, 1)
                    VE.n += 1
                    ve_scan[ci] = VE.n
                po_after_q[q] = PO.n
                bank_rel[cb] = (VE, VE.n)

            @blk.scalar
            def _(s):
                s.wait_ge(VE.h, VE.n)
                s.activation(out=ysq, in_=ybuf, func=AF.Square)
                s.last.then_inc(AC.h, 1)
            AC.n += 1

            @blk.vector
            def _(v):
                v.wait_ge(AC.h, AC.n)
                v.reduce_sum(yds[:, 0:B], ybuf, axis=AX.X).then_inc(VE.h, 1)
                v.reduce_sum(yds[:, B:2 * B], ysq, axis=AX.X).then_inc(VE.h, 1)
                VE.n += 2

            @blk.tensor
            def _(t):
                t.wait_ge(VE.h, VE.n)
                te_take(t, 5)
                t.matmul(pb[5][0:1, 0:2 * B], onesc_sb[:], yds[:], start=True, stop=True)
                t.last.then_inc(TE.h, 1)
            TE.n += 1

            @blk.vector
            def _(v):
                v.wait_ge(TE.h, TE.n)
                mu = strow_sb.ap()[:, 0:B]
                m2 = strow_sb.ap()[:, B:2 * B]
                var = strow_sb.ap()[:, 2 * B:3 * B]
                v.tensor_scalar(mu, pb[5][0:1, 0:B], 1.0 / DI, None, OP.mult).then_inc(VE.h, 1)
                v.tensor_scalar(m2, pb[5][0:1, B:2 * B], 1.0 / DI, None, OP.mult).then_inc(VE.h, 1)
                VE.n += 2
                v.wait_ge(VE.h, VE.n)
                v.tensor_tensor(var, mu, mu, OP.mult).then_inc(VE.h, 1)
                VE.n += 1
                v.wait_ge(VE.h, VE.n)
                v.tensor_tensor(var, m2, var, OP.subtract).then_inc(VE.h, 1)
                VE.n += 1
            bank_rel[5] = (VE, VE.n)

            @blk.scalar
            def _(s):
                s.wait_ge(VE.h, VE.n)
                var = strow_sb.ap()[:, 2 * B:3 * B]
                lnv = strow_sb.ap()[:, B:2 * B]
                rstd = strow_sb.ap()[:, 3 * B:4 * B]
                s.activation(out=lnv, in_=var, func=AF.Ln, bias=1e-5).then_inc(AC.h, 1)
                AC.n += 1
                s.wait_ge(AC.h, AC.n)
                s.activation(out=rstd, in_=lnv, func=AF.Exp, scale=-0.5).then_inc(AC.h, 1)
                AC.n += 1

            @blk.tensor
            def _(t):
                t.wait_ge(AC.h, AC.n)
                te_take(t, 5)
                t.matmul(pb[5][:, 0:B], onesr_sb[:], strow_sb.ap()[:, 0:B], start=True, stop=True)
                t.matmul(pb[5][:, B:2 * B], onesr_sb[:], strow_sb.ap()[:, 3 * B:4 * B],
                         start=True, stop=True)
                t.last.then_inc(TE.h, 1)
            TE.n += 1

            @blk.vector
            def _(v):
                v.wait_ge(TE.h, TE.n)
                mub = pb[5].ap()[:, 0:B][:, :, None].broadcast_to([128, B, 16])
                rsb = pb[5].ap()[:, B:2 * B][:, :, None].broadcast_to([128, B, 16])
                v.tensor_tensor(yln[:], ybuf[:], mub, OP.subtract).then_inc(VE.h, 1)
                VE.n += 1
                v.wait_ge(VE.h, VE.n)
                v.tensor_tensor(ybuf[:], yln[:], rsb, OP.mult).then_inc(VE.h, 1)
                VE.n += 1
                v.wait_ge(VE.h, VE.n)
                for dc in range(16):
                    v.tensor_scalar(yln[:, :, dc], ybuf[:, :, dc], lng_g.ap()[:, dc:dc + 1],
                                    lnb_g.ap()[:, dc:dc + 1], OP.mult, OP.add).then_inc(VE.h, 1)
                    VE.n += 1
            bank_rel[5] = (VE, VE.n)

            @blk.vector
            def _(v):
                v.wait_ge(VE.h, VE.n)
                for dc in range(16):
                    v.scalar_tensor_tensor(xc_r[:, dc, :], u_r[:, dc, :], D_g.ap()[:, dc:dc + 1],
                                           yln[:, :, dc], OP.mult, OP.add).then_inc(VE.h, 1)
                    VE.n += 1

            @blk.gpsimd
            def _(g):
                g.wait_ge(VE.h, VE.n)
                g.tensor_tensor(y2bf[:], xc_r[:], zs_r[:], OP.mult)
                g.last.then_inc(PO.h, 1)
            PO.n += 1

            opch = [(half, kc) for half in range(2) for kc in range(16)]
            op_te_counts = {}
            for i, (half, kc) in enumerate(opch):
                @blk.sync
                def _(sync, i=i, half=half, kc=kc):
                    if i >= 6 and (i - 6) in op_te_counts:
                        sync.wait_ge(TE.h, op_te_counts[i - 6])
                    sync.dma_start(out=opw_c[i % 6],
                                   in_=opw_bf[kc * 128:(kc + 1) * 128,
                                              half * 512:(half + 1) * 512]).then_inc(LD.h, 16)
                LD.n += 16

                @blk.tensor
                def _(t, i=i, half=half, kc=kc):
                    t.wait_ge(LD.h, LD.n)
                    if i == 0:
                        t.wait_ge(PO.h, PO.n)
                        te_take(t, 0)
                        te_take(t, 1)
                    t.matmul(pb[half][:, 0:512], y2bf[:, kc, :], opw_c[i % 6],
                             start=(kc == 0), stop=(kc == 15))
                    t.last.then_inc(TE.h, 1)
                TE.n += 1
                op_te_counts[i] = TE.n

            @blk.scalar
            def _(s):
                s.wait_ge(TE.h, TE.n)
                s.wait_ge(STm.h, STm.n)
                s.activation(out=s_tile[:, 0:512], in_=pb[0][:, 0:512], func=AF.Copy).then_inc(AC.h, 1)
                s.activation(out=s_tile[:, 512:1024], in_=pb[1][:, 0:512], func=AF.Copy).then_inc(AC.h, 1)
                AC.n += 2
            bank_rel[0] = (AC, AC.n)
            bank_rel[1] = (AC, AC.n)

            @blk.sync
            def _(sync, tb=tb):
                sync.wait_ge(AC.h, AC.n)
                sync.dma_start(out=s_dram[tb * B:(tb + 1) * B, :], in_=s_tile).then_inc(STm.h, 16)
            STm.n += 16

        for tb in range(NB if DO_PH3 else 0):
            @blk.sync
            def _(sync, tb=tb):
                wait_all(sync)
                sync.dma_start(out=p3x, in_=x_in[tb * B:(tb + 1) * B, :]).then_inc(LD.h, 16)
                sync.dma_start(out=p3g, in_=g_dram[tb * B:(tb + 1) * B, :]).then_inc(LD.h, 16)
                sync.dma_start(out=p3s, in_=s_dram[tb * B:(tb + 1) * B, :]).then_inc(LD.h, 16)
            LD.n += 48

            @blk.vector
            def _(v):
                v.wait_ge(LD.h, LD.n)
                mu = small_sb.ap()[:, 0:1]
                v.reduce_sum(mu, p3g[:, None, :], axis=AX.X).then_inc(VE.h, 1)
                VE.n += 1
                v.wait_ge(VE.h, VE.n)
                v.tensor_scalar(mu, mu, 1.0 / D, None, OP.mult).then_inc(VE.h, 1)
                VE.n += 1
                v.wait_ge(VE.h, VE.n)
                v.tensor_scalar(p3t, p3g, mu, None, OP.subtract).then_inc(VE.h, 1)
                VE.n += 1

            @blk.scalar
            def _(s):
                s.wait_ge(VE.h, VE.n)
                s.activation(out=p3o, in_=p3t, func=AF.Square)
                s.last.then_inc(AC.h, 1)
            AC.n += 1

            @blk.vector
            def _(v):
                v.wait_ge(AC.h, AC.n)
                var = small_sb.ap()[:, 2:3]
                v.reduce_sum(var, p3o[:, None, :], axis=AX.X).then_inc(VE.h, 1)
                VE.n += 1
                v.wait_ge(VE.h, VE.n)
                v.tensor_scalar(var, var, 1.0 / D, None, OP.mult).then_inc(VE.h, 1)
                VE.n += 1

            @blk.scalar
            def _(s):
                s.wait_ge(VE.h, VE.n)
                var = small_sb.ap()[:, 2:3]
                lnv = small_sb.ap()[:, 4:5]
                rstd = small_sb.ap()[:, 3:4]
                s.activation(out=lnv, in_=var, func=AF.Ln, bias=1e-5).then_inc(AC.h, 1)
                AC.n += 1
                s.wait_ge(AC.h, AC.n)
                s.activation(out=rstd, in_=lnv, func=AF.Exp, scale=-0.5).then_inc(AC.h, 1)
                AC.n += 1

            @blk.vector
            def _(v):
                v.wait_ge(AC.h, AC.n)
                rstd = small_sb.ap()[:, 3:4]
                v.tensor_scalar(p3o, p3t, rstd, None, OP.mult).then_inc(VE.h, 1)
                VE.n += 1
                v.wait_ge(VE.h, VE.n)
                v.tensor_tensor(p3t, p3o, gam_sb[:], OP.mult).then_inc(VE.h, 1)
                VE.n += 1
                v.wait_ge(VE.h, VE.n)
                v.tensor_tensor(p3o, p3t, bet_sb[:], OP.add).then_inc(VE.h, 1)
                VE.n += 1

            @blk.scalar
            def _(s):
                s.wait_ge(VE.h, VE.n)
                s.activation(out=p3t, in_=p3o, func=AF.Silu)
                s.last.then_inc(AC.h, 1)
            AC.n += 1

            @blk.vector
            def _(v):
                v.wait_ge(AC.h, AC.n)
                v.tensor_tensor(p3o, p3s, p3t, OP.mult).then_inc(VE.h, 1)
                VE.n += 1
                v.wait_ge(VE.h, VE.n)
                v.tensor_tensor(p3o, p3o, p3x, OP.add).then_inc(VE.h, 1)
                VE.n += 1

            @blk.sync
            def _(sync, tb=tb):
                sync.wait_ge(VE.h, VE.n)
                sync.dma_start(out=out[tb * B:(tb + 1) * B, :], in_=p3o).then_inc(STm.h, 16)
            STm.n += 16

        @blk.sync
        def _(sync):
            sync.wait_ge(STm.h, STm.n)

    return nc


def _prep_inputs(inputs):
    f = np.float32
    ip = np.ascontiguousarray(inputs["L0_attn_in_proj_w"].T).astype(f)
    op = np.ascontiguousarray(inputs["L0_attn_out_proj_w"].T).astype(f)
    xp = inputs["L0_attn_x_proj_w"]
    xpw = np.zeros((DI, 80), f)
    xpw[:, 0] = xp[0]
    xpw[:, 32:48] = xp[1:17].T
    xpw[:, 64:80] = xp[17:33].T
    dtwr = np.ascontiguousarray(inputs["L0_attn_dt_proj_w"][:, 0][None, :]).astype(f)
    gw = np.ascontiguousarray(inputs["L0_ffn_gate_w"].T).astype(f)

    def grid(vec):
        return np.ascontiguousarray(np.asarray(vec, f).reshape(16, 128).T)

    convw = np.ascontiguousarray(
        np.asarray(inputs["L0_attn_conv_w"], f)[:, 0, :].reshape(16, 128, 4).transpose(1, 0, 2))

    sel = np.zeros((S, S * 128), f)
    for s in range(S):
        sel[s, s * 128:(s + 1) * 128] = 1.0

    cconst = np.zeros((128, 8), f)
    cconst[:, 0] = 1.0
    cconst[:, 1] = 1e-5
    cconst[:, 2] = -0.5
    cconst[:, 3] = -1.0

    common = {
        "ipwT": ip, "opwT": op, "xpw": xpw, "dtw": dtwr, "gwT": gw,
        "w1": np.ascontiguousarray(inputs["L0_ffn_expert_w1"], f),
        "w2": np.ascontiguousarray(inputs["L0_ffn_expert_w2"], f),
        "g_convb": grid(inputs["L0_attn_conv_b"]),
        "g_dtb": grid(inputs["L0_attn_dt_proj_b"]),
        "g_D": grid(inputs["L0_attn_D"]),
        "g_lng": grid(inputs["L0_attn_ln_g"]),
        "g_lnb": grid(inputs["L0_attn_ln_b"]),
        "g_convw": convw,
        "alog": np.asarray(inputs["L0_attn_A_log"], f)[None, :],
        "gamma_b": np.ascontiguousarray(np.broadcast_to(np.asarray(inputs["L0_ffn_ln_gamma"], f), (128, D))),
        "beta_b": np.ascontiguousarray(np.broadcast_to(np.asarray(inputs["L0_ffn_ln_beta"], f), (128, D))),
        "ones_col": np.ones((128, 1), f),
        "ones_row": np.ones((1, 128), f),
        "sel16": sel,
        "cconst": cconst,
        "ident_in": np.eye(128, dtype=f),
    }
    x = np.asarray(inputs["L0_attn_x"], f)
    in_maps = []
    for b in range(NCORE):
        m = dict(common)
        m["x"] = np.ascontiguousarray(x[b])
        in_maps.append(m)
    return in_maps


def kernel(**inputs):
    in_maps = _prep_inputs(inputs)
    nc = build(debug=False)
    res = run_bass_kernel_spmd(nc, in_maps, core_ids=list(range(NCORE)))
    outs = [res.results[b]["out"] for b in range(NCORE)]
    return np.stack(outs, axis=0).astype(np.float32)
